# version 15
# baseline (speedup 1.0000x reference)
import contextlib
import numpy as np
import concourse.bass as bass
import concourse.mybir as mybir
from concourse.bass_utils import run_bass_kernel_spmd

F32 = mybir.dt.float32
BF16 = mybir.dt.bfloat16
I32 = mybir.dt.int32
AF = mybir.ActivationFunctionType
ALU = mybir.AluOpType

D = 2048
TOK = 2048
NCH = 16
KCN = 16
HEADS = 16
INTER = 5504
ICN = 43
EPS = 1e-5
SWP = 8
SWN = D // SWP
IDXC = 256
IDXD = 688
NCORES = 8
FBLK = 2
FBT = TOK // FBLK

STAGE = "full"
NO_GATHER = False
GATHER_LIMIT = 10 ** 9
GATHER_NO_DMA = False
GATHER_LEVEL = 4
DUMP_CCIN = False
FFN_SERIAL = False
FFN_BARRIER = False
X1_SETTLE = False
A1_CHUNKS = NCH
ORDER = ["a1", "a2a", "a2b", "a2c", "a4a", "x1", "ln2", "ffw", "ffh", "ffl", "full"]

LF = 0
PBF = 32768
SC = 65536
SD = 98304
ABLEN = 102400
G_FIDX = SC
G_ACC0 = SC + 2112
G_ACC1 = SC + 2816
G_PLN = SC + 3520
G_STG = SC + 4224
SE_LN1 = SC + 4736
SE_TR1 = SC + 6784
SE_V = SC + 8832
SE_ELU = SC + 9344
SC_X = SC + 9856
SC_KVS = SC + 18048
WB_AB = SC + 22144
SC_U = SC
SC_LN2F = SC
SD_U = SD
FF_DW = SD
SE_SG = SD + 2048
LF_PFM = LF
LF_TBF = LF + 4096
LF_TRASH = LF + 16384
LF_TRASH2 = LF + 4096
LF_GB = LF + 8192
LF_XQ = LF + 12288
LF_X1Q = LF + 14336
LF_LN2S = LF + 16384
LF_X1C = LF + 18432
AT_BF = LF + 18432
WB_WO = LF + 22528
LF_KVB = LF + 30720
FF_H = 0
FF_GB = 45056
FF_X1 = 53248
FF_X2 = 61440


class Ctr:
    __slots__ = ("h", "n")

    def __init__(self, h):
        self.h = h
        self.n = 0

    def inc(self, k=1):
        self.n += k
        return self.n


def kvcol(h):
    return (h % 2) * 1024 + (h // 2) * 128


def build():
    nc = bass.Bass("TRN2", target_bir_lowering=False,
                   detect_race_conditions=False)
    dt = nc.dram_tensor
    x_in = dt("x", [TOK, D], F32, kind="ExternalInput").ap()
    wmats = {nm: dt(nm, [D, D], F32, kind="ExternalInput").ap()
             for nm in ("wqk", "wv", "wu", "wo")}
    vecs = {nm: dt(nm, [1, D], F32, kind="ExternalInput").ap()
            for nm in ("bqk", "bv", "bu", "bo", "lng", "lnb")}
    lut = dt("lut", [1, 16], F32, kind="ExternalInput").ap()
    gidx = dt("gidx", [INTER, IDXC], I32, kind="ExternalInput").ap()
    uidx = dt("uidx", [INTER, IDXC], I32, kind="ExternalInput").ap()
    didx = dt("didx", [D, IDXD], I32, kind="ExternalInput").ap()
    y = dt("y", [TOK, D], F32, kind="ExternalOutput").ap()
    x1d = dt("x1d", [TOK, D], F32, kind="Internal").ap()
    SHARD = 2 * ICN * 128 * 128
    cc_in = dt("cc_in", [3, SHARD], BF16, kind="Internal")
    cc_out = dt("cc_out", [3 * NCORES, SHARD], BF16, kind="Internal")
    scr_dbg = dt("scr_dbg", [128, 128], BF16, kind="Internal")
    kv_in = dt("kv_in", [128, D], F32, kind="Internal")
    kv_out = dt("kv_out", [128, D], F32, kind="Internal")

    es = contextlib.ExitStack()
    with es:
        AB = es.enter_context(nc.sbuf_tensor("AB", [128, ABLEN], BF16))
        biasT = es.enter_context(nc.sbuf_tensor("biasT", [128, D], BF16))
        ident = es.enter_context(nc.sbuf_tensor("ident", [128, 128], BF16))
        ones = es.enter_context(nc.sbuf_tensor("ones", [128, 128], BF16))
        lutbc = es.enter_context(nc.sbuf_tensor("lutbc", [128, 16], F32))
        stats = es.enter_context(nc.sbuf_tensor("stats", [128, 64], F32))
        PA = es.enter_context(nc.psum_tensor("PA", [128, 1024], F32))
        PBp = es.enter_context(nc.psum_tensor("PBp", [128, 2048], F32))
        PT = es.enter_context(nc.psum_tensor("PT", [128, 2048], BF16))

        sem_names = ["cst", "cstC", "cnv", "gac", "tpe", "tev", "mm",
                     "evA", "evV", "evb", "stA", "var", "ln", "nm", "fz",
                     "cc", "kv", "pd", "sx2", "sbo", "sgb", "yod", "evE"]
        S = {nm: Ctr(es.enter_context(nc.semaphore(nm))) for nm in sem_names}

        def slot_sems(name, k):
            return [Ctr(es.enter_context(nc.semaphore(f"{name}{i}")))
                    for i in range(k)]

        SX = slot_sems("sx", 2)
        SW = slot_sems("sw", 2)
        SIDX = slot_sems("sidx", 2)
        SSH = slot_sems("ssh", 4)
        FFG = slot_sems("ffg", 2)
        FFD = slot_sems("ffd", 2)
        XQ = slot_sems("xq", 4)
        X1L = slot_sems("x1l", 2)
        SX1 = slot_sems("sx1", 2)
        SYO = slot_sems("syo", 2)
        mm, evA, evV, evb = S["mm"], S["evA"], S["evV"], S["evb"]
        tpe, tev, nmc, pd = S["tpe"], S["tev"], S["nm"], S["pd"]

        Q = {e: [] for e in ("sync", "tensor", "vector", "scalar", "gpsimd")}

        def em(eng, fn):
            Q[eng].append(fn)

        def ab(off, ln):
            return AB[:, off:off + ln]

        def abf(off, ln):
            return AB[:, off:off + ln].bitcast(F32)

        cst = S["cst"]

        def g0(g):
            g.dma_start(out=biasT[0:1, 0:D], in_=vecs["bqk"][:, :]).then_inc(cst.h, 16)
            g.dma_start(out=biasT[32:33, 0:D], in_=vecs["bv"][:, :]).then_inc(cst.h, 16)
            g.dma_start(out=biasT[64:65, 0:D], in_=vecs["bu"][:, :]).then_inc(cst.h, 16)
            g.dma_start(out=lutbc[:], in_=lut.broadcast_to([128, 16])).then_inc(cst.h, 16)
            g.memset(ones[:], 1.0)
            g.memset(ident[:], 0.0).then_inc(S["cstC"].h, 1)
            g.wait_ge(cst.h, 16 * 4)
            g.wait_ge(S["cstC"].h, 1)
            g.affine_select(out=ident[:], in_=ident[:], compare_op=ALU.not_equal,
                            fill=1.0, base=0, pattern=[[-1, 128]],
                            channel_multiplier=1).then_inc(S["cstC"].h, 1)
        em("gpsimd", g0)
        cst.inc(16 * 4)
        S["cstC"].inc(2)
        CST = cst.n
        CSTC = S["cstC"].n

        cnv = S["cnv"]

        tg_n = [0]

        def transpose_block(srcs, waits=(), batch_dst=None, tile_evac=None):
            g = tg_n[0]
            tg_n[0] += 1
            col0 = (g % 2) * 1024
            t0 = tpe.n
            for i, s in enumerate(srcs):
                if isinstance(s, tuple):
                    s, rows = s
                else:
                    rows = 128

                def f(te, i=i, s=s, g=g, col0=col0,
                      waits=tuple(waits) if i == 0 else ()):
                    for (h, v) in waits:
                        te.wait_ge(h, v)
                    if i == 0 and g >= 2:
                        te.wait_ge(tev.h, g - 1)
                    if g == 0 and i == 0:
                        te.wait_ge(cst.h, CST)
                        te.wait_ge(S["cstC"].h, CSTC)
                    te.transpose(PT[:, col0 + i * 128:col0 + (i + 1) * 128],
                                 s, ident[:]).then_inc(tpe.h, 1)
                em("tensor", f)
                tpe.inc()
            tneed = tpe.n
            n = len(srcs)
            if batch_dst is not None:
                def fe(sc, bd=batch_dst, col0=col0, n=n, tneed=tneed):
                    sc.wait_ge(tpe.h, tneed)
                    sc.activation(out=bd, in_=PT[:, col0:col0 + n * 128],
                                  func=AF.Copy).then_inc(tev.h, 1)
                em("scalar", fe)
            else:
                def fe(sc, col0=col0, n=n, tneed=tneed, te_=tile_evac):
                    sc.wait_ge(tpe.h, tneed)
                    for i in range(n):
                        te_(sc, i, col0 + i * 128, i == n - 1)
                em("scalar", fe)
            tev.inc()
            return tev.n

        lnslot = [0]

        def ln_chunk(src_f32, dst_bf16, trash_off, x_wait=None,
                     emit_norm=True, norm_wait=None):
            cid = lnslot[0]
            lnslot[0] += 1
            s = cid % 8
            st = stats[:, 8 * s:8 * s + 8]
            stA, var, ln = S["stA"], S["var"], S["ln"]

            def fa(sc, src=src_f32, st=st, xw=x_wait, cid=cid, to=trash_off):
                if cid >= 8:
                    sc.wait_ge(nmc.h, cid - 7)
                if xw is not None:
                    sc.wait_ge(xw[0], xw[1])
                sc.activation(out=ab(to, 2048), in_=src, func=AF.Copy,
                              scale=1.0 / D, accum_out=st[:, 0:1])
                sc.activation(out=ab(to, 2048), in_=src, func=AF.Square,
                              scale=1.0 / float(np.sqrt(D)),
                              accum_out=st[:, 1:2])
                sc.activation(out=st[:, 2:3], in_=st[:, 0:1],
                              func=AF.Square).then_inc(stA.h, 1)
            em("scalar", fa)
            stA.inc()
            vA = stA.n

            def fp(g, st=st, vA=vA):
                g.wait_ge(stA.h, vA)
                g.tensor_scalar(out=st[:, 3:4], in0=st[:, 2:3], scalar1=-1.0,
                                scalar2=float(EPS), op0=ALU.mult,
                                op1=ALU.add).then_inc(var.h, 1)
            em("vector", fp)
            var.inc()
            vV = var.n

            def fb(sc, st=st, vV=vV):
                sc.wait_ge(var.h, vV)
                sc.activation(out=st[:, 4:5], in_=st[:, 1:2], func=AF.Ln,
                              bias=st[:, 3:4])
                sc.activation(out=st[:, 5:6], in_=st[:, 4:5], func=AF.Exp,
                              scale=-0.5).then_inc(ln.h, 1)
            em("scalar", fb)
            ln.inc()
            lV = ln.n
            if not emit_norm:
                return lV, st

            def fv(v, src=src_f32, dst=dst_bf16, st=st, lV=lV,
                   nw=norm_wait):
                v.wait_ge(ln.h, lV)
                if nw is not None:
                    v.wait_ge(nw[0], nw[1])
                v.tensor_scalar(out=dst, in0=src, scalar1=st[:, 0:1],
                                scalar2=st[:, 5:6], op0=ALU.subtract,
                                op1=ALU.mult)
                v.tensor_copy(out=stats[:, 56:60], in_=dst[:, 0:4]
                              ).then_inc(nmc.h, 1)
            em("vector", fv)
            nmc.inc()
            return nmc.n

        pa_group = [0]
        pa_guard = [None, None]

        def pa_alloc():
            gi = pa_group[0]
            pa_group[0] += 1
            slot = gi % 2
            return gi, slot, pa_guard[slot]

        def pa_free(slot, ctr):
            ctr.inc()
            pa_guard[slot] = (ctr.h, ctr.n)
            return ctr.n

        gac = S["gac"]
        units = []
        for m, src in ((0, gidx), (1, uidx)):
            for pair in range(22):
                o0 = 2 * pair
                n_oc = 2 if o0 + 1 < ICN else 1
                loads = [(src[(o0 + i) * 128:(o0 + i + 1) * 128, :],
                          i * IDXC, IDXC) for i in range(n_oc)]
                tiles = []
                for i in range(n_oc):
                    for sub in range(2):
                        tiles.append((i * IDXC + sub * 128, 128,
                                      ("tile", m, sub * ICN + (o0 + i))))
                units.append((loads, n_oc * IDXC, tiles))
        for oc in range(16):
            loads = [(didx[oc * 128:(oc + 1) * 128, :], 0, IDXD)]
            tiles = [(icl * 128, 128 if icl < 5 else 48,
                      ("down", icl * 128, oc * 128)) for icl in range(6)]
            units.append((loads, IDXD, tiles))
        NUNITS = len(units)
        g_next_load = [0]
        g_cmp = [0]
        g_tile = [0]
        g_acc_tpe = {}
        pend_tp = []
        unit_loadv = {}

        def emit_load(j):
            loads, fd, _ = units[j]
            sidx = SIDX[j % 2]

            def fg(g, loads=loads, j=j, sidx=sidx):
                if j >= 3:
                    g.wait_ge(gac.h, j - 2)
                fb2 = G_FIDX + (j % 3) * 704
                for (ap_, co, cl) in loads:
                    g.dma_start(out=AB[:, fb2 + co:fb2 + co + cl],
                                in_=ap_).then_inc(sidx.h, 16)
            em("gpsimd", fg)
            sidx.inc(16 * len(loads))
            unit_loadv[j] = sidx.n

        def emit_dummy_load(j):
            sidx = SIDX[j % 2]

            def fg(g, j=j, sidx=sidx):
                g.wait_ge(gac.h, j - 2)
                g.dma_start(out=AB[:, G_FIDX + (j % 3) * 704:
                                   G_FIDX + (j % 3) * 704 + 64],
                            in_=gidx[0:128, 0:64]).then_inc(sidx.h, 16)
            em("gpsimd", fg)
            sidx.inc(16)
            unit_loadv[j] = sidx.n

        def drain_cmp(k=1):
            if NO_GATHER:
                return
            for _ in range(k):
                if g_cmp[0] >= min(NUNITS, GATHER_LIMIT):
                    return
                ui = g_cmp[0]
                g_cmp[0] += 1
                while g_next_load[0] <= min(ui + 2, NUNITS + 1):
                    j = g_next_load[0]
                    if j < NUNITS:
                        emit_load(j)
                    else:
                        emit_dummy_load(j)
                    g_next_load[0] += 1
                loads, fd, tiles = units[ui]
                sidx = SIDX[ui % 2]
                iV = unit_loadv[ui + 2]
                accb = G_ACC0 if ui % 2 == 0 else G_ACC1
                if GATHER_LEVEL < 2:
                    continue

                def fv(v, ui=ui, fd=fd, iV=iV, sidx=sidx, accb=accb,
                       ptv=g_acc_tpe.get(ui - 2, 0)):
                    v.wait_ge(sidx.h, iV)
                    if ptv:
                        v.wait_ge(tpe.h, ptv)
                    if ui == 0:
                        v.wait_ge(cst.h, CST)
                    fb2 = G_FIDX + (ui % 3) * 704
                    acc = ab(accb, fd)
                    pln = ab(G_PLN, fd)
                    src = ab(fb2, fd)
                    for j in range(16):
                        dst = acc if j == 0 else pln
                        v.tensor_scalar(out=dst, in0=src, scalar1=float(j),
                                        scalar2=lutbc[:, j:j + 1],
                                        op0=ALU.is_equal, op1=ALU.mult)
                        if j > 0:
                            v.tensor_tensor(out=acc, in0=acc, in1=pln,
                                            op=ALU.add)
                    v.tensor_copy(out=ab(G_PLN, 8), in_=ab(accb, 8)
                                  ).then_inc(gac.h, 1)
                em("vector", fv)
                gac.inc()
                pend_tp.append((ui, gac.n, tiles, accb))

        def drain_tp(k=1, lag=2):
            if NO_GATHER:
                return
            for _ in range(k):
                if len(pend_tp) <= lag:
                    return
                ui, gv, tiles, accb = pend_tp.pop(0)
                first = True
                if GATHER_LEVEL < 3:
                    g_acc_tpe[ui] = tpe.n
                    continue
                for b0 in range(0, len(tiles), 4):
                    blk_tiles = tiles[b0:b0 + 4]
                    evinfo = []
                    for (scol, rows, dst) in blk_tiles:
                        gt = g_tile[0]
                        g_tile[0] += 1
                        ssh = SSH[gt % 4]
                        evinfo.append((G_STG + (gt % 4) * 128, rows, dst,
                                       gt, ssh, ssh.n))
                        ssh.inc(16)

                    def mk_te(evinfo):
                        def te_(sc, i, ptcol, is_last):
                            stg, rows, dst, gt, ssh, sshv = evinfo[i]
                            if gt >= 4:
                                sc.wait_ge(ssh.h, sshv)
                            ins = sc.activation(
                                out=AB[0:rows, stg:stg + 128],
                                in_=PT[0:rows, ptcol:ptcol + 128],
                                func=AF.Copy)
                            if is_last:
                                ins.then_inc(tev.h, 1)
                            if dst[0] == "down":
                                r0, c0 = dst[1], dst[2]
                                dd = cc_in.ap()[2:3, :].rearrange(
                                    "a (r c) -> (a r) c", r=IDXD)[
                                    r0:r0 + rows, c0:c0 + 128]
                            else:
                                _, mi, ti = dst
                                dd = cc_in.ap()[mi:mi + 1,
                                                ti * 16384:(ti + 1) * 16384
                                                ].rearrange(
                                    "a (p e) -> (a p) e", p=128)[0:rows, :]
                            if GATHER_NO_DMA:
                                sc.dma_start(out=scr_dbg.ap()[0:rows, :],
                                             in_=AB[0:rows, stg:stg + 128]
                                             ).then_inc(ssh.h, 16)
                            else:
                                sc.dma_start(out=dd,
                                             in_=AB[0:rows, stg:stg + 128]
                                             ).then_inc(ssh.h, 16)
                        return te_

                    srcs = [(ab(accb + scol, 128), rows)
                            for (scol, rows, dst) in blk_tiles]
                    if GATHER_LEVEL < 4:
                        def te_noop(sc, i, ptcol, is_last, n=len(srcs)):
                            if is_last:
                                sc.activation(out=ab(G_STG, 128),
                                              in_=PT[:, ptcol:ptcol + 128],
                                              func=AF.Copy).then_inc(tev.h, 1)
                        transpose_block(srcs,
                                        waits=[(gac.h, gv)] if first else (),
                                        tile_evac=te_noop)
                    else:
                        transpose_block(srcs,
                                        waits=[(gac.h, gv)] if first else (),
                                        tile_evac=mk_te(evinfo))
                    first = False
                g_acc_tpe[ui] = tpe.n

        stage_n = ORDER.index(STAGE)

        class StopEmit(Exception):
            pass

        def dbg_dump(region_off, wait_list, n_el=32768, f32=False):
            def fd(g, ro=region_off, wl=tuple(wait_list), n_el=n_el, f32=f32):
                for (h, v) in wl:
                    g.wait_ge(h, v)
                if f32:
                    na = n_el // 2 // 2048
                    yv = y.rearrange("(a p) d -> p a d", p=128)[:, 0:na, :]
                    sv = abf(ro, n_el).rearrange("p (a d) -> p a d", d=2048)
                else:
                    na = n_el // 2048
                    yv = y.rearrange("(a p) d -> p a d", p=128)[:, 0:na, :]
                    sv = ab(ro, n_el).rearrange("p (a d) -> p a d", d=2048)
                g.dma_start(out=yv, in_=sv).then_inc(S["yod"].h, 16)
                g.wait_ge(S["yod"].h, 16)
            em("gpsimd", fd)
            S["yod"].inc(16)
            raise StopEmit()

        try:
            sd_release = {}
            sweep_swv = {}

            def emit_sweep_load(wname, s, swn, wb_base, wb_sz,
                                extra_w_wait=None, wb_map=None,
                                rel_override=None):
                wb = (wb_map[s] if wb_map is not None
                      else wb_base + (s % 2) * wb_sz)
                rel = (rel_override if rel_override is not None
                       else sd_release.get(wb))
                sw = SW[s % 2]

                def fw(g, s=s, wb=wb, rel=rel, wname=wname, swn=swn,
                       wb_sz=wb_sz, eww=extra_w_wait, sw=sw):
                    if eww is not None and s < 2:
                        g.wait_ge(eww[0], eww[1])
                    if rel is not None:
                        g.wait_ge(rel[0], rel[1])
                    src = wmats[wname].rearrange("(kc p) d -> p kc d",
                                                 p=128)[
                        :, :, s * swn:(s + 1) * swn]
                    g.dma_start(out=AB[:, wb:wb + wb_sz].rearrange(
                        "p (kc n) -> p kc n", n=swn), in_=src
                        ).then_inc(sw.h, 16)
                    g.dma_start(out=AB[:, wb:wb + 64],
                                in_=src[:, 0, 0:64]).then_inc(sw.h, 16)
                em("gpsimd", fw)
                sw.inc(32)
                sweep_swv[(wname, s)] = sw.n


            emit_sweep_load("wqk", 0, SWN, WB_AB, 4096)
            emit_sweep_load("wqk", 1, SWN, WB_AB, 4096)

            lf_tev = []
            prev_tr = [0]
            for tc in range(A1_CHUNKS):
                xb = SC_X + (tc % 2) * 4096
                sx = SX[tc % 2]

                def fx(sy, tc=tc, xb=xb, sx=sx):
                    if tc >= 2:
                        sy.wait_ge(nmc.h, tc - 1)
                    sy.dma_start(out=abf(xb, 4096),
                                 in_=x_in[tc * 128:(tc + 1) * 128, :]
                                 ).then_inc(sx.h, 16)
                em("sync", fx)
                sx.inc(16)
                nmv = ln_chunk(abf(xb, 4096), ab(SE_LN1, 2048), SE_TR1,
                               x_wait=(sx.h, sx.n),
                               norm_wait=(tpe.h, prev_tr[0]) if tc else None)
                ARB = AB.ap().rearrange("p (a r) -> p a r", r=2048)
                for k0 in range(0, KCN, 4):
                    transpose_block(
                        [ab(SE_LN1 + kc * 128, 128)
                         for kc in range(k0, k0 + 4)],
                        waits=[(nmc.h, nmv)] if k0 == 0 else (),
                        batch_dst=ARB[:, (LF // 2048) + k0:
                                      (LF // 2048) + k0 + 4,
                                      tc * 128:(tc + 1) * 128])
                lf_tev.append(tev.n)
                prev_tr[0] = tpe.n
            if stage_n == 0:
                dbg_dump(LF, [(tev.h, tev.n)], n_el=A1_CHUNKS * 2048)

            def proj_sweeps(wname, bias_part, post, lhsT_off=LF, ready=None,
                            swp=SWP, wb_base=WB_AB, wb_sz=4096, extra_w_wait=None,
                            lhsT_tcmajor=False, s_order=None, wb_map=None):
                swn = D // swp
                pend = []
                for s in (range(swp) if s_order is None else s_order):
                    wb = (wb_map[s] if wb_map is not None
                          else wb_base + (s % 2) * wb_sz)
                    sw = SW[s % 2]
                    if (wname, s) not in sweep_swv:
                        emit_sweep_load(wname, s, swn, wb_base, wb_sz,
                                        extra_w_wait, wb_map=wb_map)
                    swv = sweep_swv[(wname, s)]
                    for tc in range(NCH):
                        while pend:
                            em("tensor", pend.pop(0))
                        gi, slot, guard = pa_alloc()
                        pa = PA[:, slot * 512:slot * 512 + swn]
                        rdy = ready[tc] if ready is not None else None

                        def fm(te, tc=tc, s=s, wb=wb, pa=pa, swv=swv,
                               guard=guard, bp=bias_part, lo=lhsT_off, rdy=rdy,
                               swn=swn, sw=sw, tcm=lhsT_tcmajor):
                            te.wait_ge(sw.h, swv)
                            if rdy is not None:
                                te.wait_ge(tev.h, rdy)
                            if guard is not None:
                                te.wait_ge(guard[0], guard[1])
                            te.matmul(pa, ones[bp:bp + 1, :],
                                      biasT[bp:bp + 1, s * swn:(s + 1) * swn],
                                      start=True, stop=False)
                            ins = None
                            for kc in range(KCN):
                                if tcm:
                                    lt = lo + tc * 2048 + kc * 128
                                else:
                                    lt = lo + kc * 2048 + tc * 128
                                ins = te.matmul(
                                    pa, ab(lt, 128),
                                    AB[:, wb + kc * swn:wb + (kc + 1) * swn],
                                    start=False, stop=(kc == KCN - 1))
                            ins.then_inc(mm.h, 1)
                        em("tensor", fm)
                        mm.inc()
                        d = post(tc, s, pa, mm.n, slot, swn)
                        if d is not None:
                            pend.append(d)
                    sd_release[wb] = (mm.h, mm.n)
                while pend:
                    em("tensor", pend.pop(0))

            pdv = {}

            def post_p(tc, s, pa, mmv, slot, swn):
                dst = ab(PBF + tc * 2048 + s * swn, swn)
                m = ab(SE_ELU, swn)
                r = ab(SE_ELU + swn, swn)

                def fv(v, pa=pa, mmv=mmv, m=m, r=r):
                    v.wait_ge(mm.h, mmv)
                    v.tensor_scalar(out=m, in0=pa, scalar1=0.0, scalar2=None,
                                    op0=ALU.min)
                    v.tensor_scalar(out=r, in0=pa, scalar1=0.0, scalar2=-1.0,
                                    op0=ALU.max, op1=ALU.add).then_inc(evV.h, 1)
                em("vector", fv)
                pa_free(slot, evV)

                def fs(sc, m=m, ev=evV.n):
                    sc.wait_ge(evV.h, ev)
                    sc.activation(out=m, in_=m, func=AF.Exp).then_inc(S["fz"].h, 1)
                em("scalar", fs)
                S["fz"].inc()
                fzv = S["fz"].n

                def fv2(v, dst=dst, m=m, r=r, fzv=fzv):
                    v.wait_ge(S["fz"].h, fzv)
                    v.tensor_tensor(out=dst, in0=r, in1=m,
                                    op=ALU.add).then_inc(pd.h, 1)
                em("vector", fv2)
                pd.inc()
                pdv[(tc, s)] = pd.n
                return None

            proj_sweeps("wqk", 0, post_p, ready=lf_tev)
            if stage_n == 1:
                dbg_dump(PBF, [(pd.h, pd.n)])
            PD_ALL = pd.n
            MM_AFTER_P = mm.n

            def f_bo(g, mg=MM_AFTER_P):
                g.wait_ge(mm.h, mg)
                g.dma_start(out=biasT[0:1, 0:D], in_=vecs["bo"][:, :]
                            ).then_inc(S["sbo"].h, 16)
            em("gpsimd", f_bo)
            S["sbo"].inc(16)
            CST_BO = S["sbo"].n

            def post_v(tc, s, pa, mmv, slot, swn):
                vb = SE_V + (tc % 2) * swn

                def fs(sc, pa=pa, mmv=mmv, vb=vb, swn=swn):
                    sc.wait_ge(mm.h, mmv)
                    sc.activation(out=ab(vb, swn), in_=pa,
                                  func=AF.Silu).then_inc(evA.h, 1)
                em("scalar", fs)
                pa_free(slot, evA)
                ev = evA.n
                last = (tc == NCH - 1 and s == SWP - 1)
                pdw = pdv[(tc, s)]

                def fk(te, tc=tc, s=s, vb=vb, ev=ev, last=last, pdw=pdw):
                    te.wait_ge(evA.h, ev)
                    te.wait_ge(pd.h, pdw)
                    ins = None
                    for h2 in range(2):
                        h = 2 * s + h2
                        ins = te.matmul(
                            PBp[:, kvcol(h):kvcol(h) + 128],
                            ab(PBF + tc * 2048 + s * swn + h2 * 128, 128),
                            AB[:, vb + h2 * 128:vb + h2 * 128 + 128],
                            start=(tc == 0), stop=(tc == NCH - 1))
                    if last:
                        ins.then_inc(S["kv"].h, 1)
                if tc % 4 == 3:
                    drain_tp(1, lag=1)
                    drain_cmp(1)
                return fk

            proj_sweeps("wv", 32, post_v, ready=lf_tev)
            S["kv"].inc()
            kv_pe_done = S["kv"].n


            def f_kvev(v):
                v.wait_ge(S["kv"].h, kv_pe_done)
                v.tensor_copy(out=abf(SC_KVS, 4096), in_=PBp[:]
                              ).then_inc(evb.h, 1)
            em("vector", f_kvev)
            evb.inc()
            evb_kv = evb.n

            def f_kvdma(sy):
                sy.wait_ge(evb.h, evb_kv)
                sy.dma_start(out=kv_in.ap()[:, :], in_=abf(SC_KVS, 4096)
                             ).then_inc(S["kv"].h, 16)
            em("sync", f_kvdma)
            S["kv"].inc(16)
            kv_dma_done = S["kv"].n
            if stage_n == 2:
                dbg_dump(SC_KVS, [(S["kv"].h, kv_dma_done)], n_el=4096,
                         f32=True)

            guardv = {}

            def post_u(tc, s, pa, mmv, slot, swn):
                dst = ab(SC_U + s * 2048 + tc * swn, swn)
                need_g = (s <= 2)
                need_kv = (8 <= s <= 10)

                def fs(sc, pa=pa, mmv=mmv, dst=dst, need_g=need_g,
                       need_kv=need_kv):
                    sc.wait_ge(mm.h, mmv)
                    if need_g:
                        sc.wait_ge(gac.h, guardv["gac"])
                        sc.wait_ge(tpe.h, guardv["tpe"])
                        for c, v in zip(SSH, guardv["ssh"]):
                            sc.wait_ge(c.h, v)
                    if need_kv:
                        sc.wait_ge(S["kv"].h, kv_dma_done)
                    sc.activation(out=dst, in_=pa, func=AF.Silu
                                  ).then_inc(evA.h, 1)
                em("scalar", fs)
                pa_free(slot, evA)
                if s >= 4 and tc in (3, 11):
                    drain_tp(1, lag=1)
                    drain_cmp(1)
                if s == 3 and tc == NCH - 1:
                    while g_cmp[0] < min(NUNITS, GATHER_LIMIT):
                        drain_tp(1, lag=1)
                        drain_cmp(1)
                    drain_tp(NUNITS + 2, lag=0)
                    guardv["gac"] = gac.n
                    guardv["tpe"] = tpe.n
                    guardv["ssh"] = [c.n for c in SSH]
                return None

            proj_sweeps("wu", 64, post_u, ready=lf_tev,
                            swp=16, wb_base=SD_U, wb_sz=2048,
                            s_order=range(15, -1, -1))
            MM_AFTER_U = mm.n
            if stage_n == 3:
                dbg_dump(SC_U, [(evA.h, evA.n)])

            WO_WB = {0: WB_WO, 1: WB_WO + 4096, 2: LF_TBF, 3: LF_XQ,
                     4: SD_U, 5: WB_WO, 6: WB_WO + 4096, 7: LF_TBF}

            def f_ccr(g):
                g.wait_ge(S["kv"].h, kv_dma_done)
                g.collective_compute(
                    "AllReduce", ALU.add,
                    replica_groups=[[0, 1], [2, 3], [4, 5], [6, 7]],
                    ins=[kv_in.ap().opt()], outs=[kv_out.ap().opt()],
                ).then_inc(S["cc"].h, 1)
            em("gpsimd", f_ccr)
            S["cc"].inc(1)

            def f_gb(g, mg=MM_AFTER_U):
                g.wait_ge(mm.h, mg)
                g.wait_ge(S["cc"].h, 1)
                g.dma_start(out=ab(LF_KVB, 2048), in_=kv_out.ap()[:, :]
                            ).then_inc(S["sgb"].h, 16)
                g.dma_start(out=ab(LF_GB, 2048),
                            in_=vecs["lng"].broadcast_to([128, D])
                            ).then_inc(S["sgb"].h, 16)
                g.dma_start(out=ab(LF_GB + 2048, 2048),
                            in_=vecs["lnb"].broadcast_to([128, D])
                            ).then_inc(S["sgb"].h, 16)
            em("gpsimd", f_gb)
            S["sgb"].inc(48)
            CST2 = S["sgb"].n

            for s_ in range(4):
                emit_sweep_load("wo", s_, SWN, WB_WO, 4096,
                                extra_w_wait=(mm.h, MM_AFTER_U), wb_map=WO_WB)
            emit_sweep_load("wo", 4, SWN, WB_WO, 4096, wb_map=WO_WB,
                            rel_override=(mm.h, MM_AFTER_U))

            def f_cca(g):
                for c, v in zip(SSH, guardv["ssh"]):
                    g.wait_ge(c.h, v)
                g.wait_ge(gac.h, guardv["gac"])
                g.collective_compute(
                    "AllGather", ALU.bypass,
                    replica_groups=[[0, 1, 2, 3, 4, 5, 6, 7]],
                    ins=[cc_in.ap().opt()], outs=[cc_out.ap().opt()],
                ).then_inc(S["cc"].h, 1)
            em("gpsimd", f_cca)
            S["cc"].inc(1)
            CC_ALL = S["cc"].n

            fzc = S["fz"]

            def f_pdall(te):
                te.wait_ge(pd.h, PD_ALL)
            em("tensor", f_pdall)
            pend_tu = []
            uln_tev = {}
            ffz_tc = {}
            for tc in range(NCH):
                while len(pend_tu) > 2:
                    pend_tu.pop(0)()
                pf = LF_PFM + (tc % 2) * 2048
                for k0 in range(0, KCN, 4):
                    transpose_block(
                        [ab(PBF + tc * 2048 + kc * 128, 128)
                         for kc in range(k0, k0 + 4)],
                        batch_dst=ab(pf + k0 * 128, 512))
                tev_need = tev.n
                gi_e = S["evE"].n

                def fe(te, tc=tc, pf=pf, tev_need=tev_need, gi_e=gi_e):
                    te.wait_ge(tev.h, tev_need)
                    te.wait_ge(S["sgb"].h, CST2)
                    if tc >= 1:
                        te.wait_ge(S["evE"].h, gi_e)
                    ins = None
                    for h in range(HEADS):
                        ins = te.matmul(
                            PBp[:, h * 128:(h + 1) * 128],
                            ab(pf + h * 128, 128),
                            AB[:, LF_KVB + kvcol(h):LF_KVB + kvcol(h) + 128],
                            start=True, stop=True)
                    ins.then_inc(mm.h, 1)
                em("tensor", fe)
                mm.inc()
                mmv = mm.n
                lV, st = ln_chunk(PBp[:], None, LF_TRASH, x_wait=(mm.h, mmv),
                                  emit_norm=False)
                atb = ab(AT_BF + (tc % 2) * 2048, 2048)
                ffz_prev = ffz_tc[tc - 2] if tc >= 2 else 0

                def fev(sc, atb=atb, fp=ffz_prev):
                    if fp:
                        sc.wait_ge(fzc.h, fp)
                    sc.activation(out=atb, in_=PBp[:], func=AF.Copy
                                  ).then_inc(S["evE"].h, 1)
                em("scalar", fev)
                S["evE"].inc()
                ub = ab(PBF + tc * 2048, 2048)

                def ff(v, tc=tc, st=st, lV=lV, ub=ub, atb=atb,
                       eE=S["evE"].n):
                    v.wait_ge(S["ln"].h, lV)
                    v.wait_ge(S["evE"].h, eE)
                    v.tensor_scalar(out=ub, in0=atb, scalar1=st[:, 0:1],
                                    scalar2=st[:, 5:6], op0=ALU.subtract,
                                    op1=ALU.mult).then_inc(evb.h, 1)
                    v.tensor_tensor(out=ub, in0=ub, in1=ab(LF_GB, 2048),
                                    op=ALU.mult)
                    v.tensor_tensor(out=ub, in0=ub, in1=ab(LF_GB + 2048, 2048),
                                    op=ALU.add)
                    AB3 = AB.ap().rearrange("p (s x) -> p s x", x=2048)
                    ub3 = AB.ap().rearrange("p (a x) -> p a x", x=128)
                    ubo = PBF + tc * 2048
                    v.tensor_tensor(
                        out=ub3[:, ubo // 128:ubo // 128 + 16, :],
                        in0=ub3[:, ubo // 128:ubo // 128 + 16, :],
                        in1=AB3[:, SC_U // 2048:SC_U // 2048 + 16,
                                tc * 128:(tc + 1) * 128],
                        op=ALU.mult)
                    v.tensor_copy(out=stats[:, 56:60], in_=ub[0:128, 0:4]
                                  ).then_inc(fzc.h, 1)
                em("vector", ff)
                evb.inc()
                fzc.inc()
                fzv = fzc.n
                ffz_tc[tc] = fzv

                def fnm(v):
                    v.engine_nop().then_inc(nmc.h, 1)
                em("vector", fnm)
                nmc.inc()

                def mk_tu(tc=tc, fzv=fzv):
                    def go():
                        for k0 in range(0, KCN, 4):
                            transpose_block(
                                [ab(PBF + tc * 2048 + kc * 128, 128)
                                 for kc in range(k0, k0 + 4)],
                                waits=[(fzc.h, fzv)] if k0 == 0 else (),
                                batch_dst=ab(PBF + tc * 2048 + k0 * 128, 512))
                        uln_tev[tc] = tev.n
                    return go
                pend_tu.append(mk_tu())
            for fcb in pend_tu:
                fcb()
            pend_tu = []
            uln_all = tev.n
            if stage_n == 4:
                dbg_dump(PBF, [(tev.h, uln_all)])

            M0 = mm.n
            for s_, relv in ((5, 16), (6, 32), (7, 48)):
                emit_sweep_load("wo", s_, SWN, WB_WO, 4096, wb_map=WO_WB,
                                rel_override=(mm.h, M0 + relv))

            def post_o(tc, s, pa, mmv, slot, swn):
                def fv(v, pa=pa, mmv=mmv, tc=tc, s=s):
                    v.wait_ge(mm.h, mmv)
                    AB3 = AB.ap().rearrange("p (k x) -> p k x", x=2048)
                    pa3 = pa.rearrange("p (i x) -> p i x", x=128)
                    v.tensor_copy(
                        out=AB3[:, SC // 2048 + 2 * s:SC // 2048 + 2 * s + 2,
                                tc * 128:(tc + 1) * 128],
                        in_=pa3).then_inc(evV.h, 1)
                em("vector", fv)
                pa_free(slot, evV)
                return None

            def f_wog(te):
                te.wait_ge(S["sbo"].h, CST_BO)
            em("tensor", f_wog)
            proj_sweeps("wo", 0, post_o, lhsT_off=PBF,
                        ready=[uln_tev[t] for t in range(NCH)],
                        wb_base=WB_WO, extra_w_wait=(mm.h, MM_AFTER_U),
                        lhsT_tcmajor=True, wb_map=WO_WB)
            EV_WO = evV.n
            MM_AFTER_WO = mm.n

            ABF3 = abf(0, ABLEN).rearrange("p (k x) -> p k x", x=128)
            X1K = (LF_X1C // 2) // 128
            for tc in range(NCH):
                nm_prev = nmc.n

                def fl(sy, tc=tc, nm_prev=nm_prev,
                       stv=SX1[(tc - 1) % 2].n):
                    if tc == 0:
                        sy.wait_ge(fzc.h, ffz_tc[NCH - 1])
                    if tc >= 1:
                        sy.wait_ge(nmc.h, nm_prev)
                        sy.wait_ge(SX1[(tc - 1) % 2].h, stv)
                    sy.dma_start(out=abf(LF_X1C, 4096),
                                 in_=x_in[tc * 128:(tc + 1) * 128, :]
                                 ).then_inc(S["sx2"].h, 16)
                em("sync", fl)
                S["sx2"].inc(16)
                xwv = S["sx2"].n

                def fadd(v, tc=tc, xwv=xwv):
                    v.wait_ge(S["sx2"].h, xwv)
                    v.wait_ge(evV.h, EV_WO)
                    AB3 = AB.ap().rearrange("p (k x) -> p k x", x=2048)
                    v.tensor_tensor(
                        out=ABF3[:, X1K:X1K + 16, :],
                        in0=ABF3[:, X1K:X1K + 16, :],
                        in1=AB3[:, SC // 2048:SC // 2048 + 16,
                                tc * 128:(tc + 1) * 128],
                        op=ALU.add).then_inc(evV.h, 1)
                em("vector", fadd)
                evV.inc()
                adv = evV.n

                def fsx(sy, tc=tc, adv=adv):
                    sy.wait_ge(evV.h, adv)
                    sy.dma_start(out=x1d[tc * 128:(tc + 1) * 128, :],
                                 in_=abf(LF_X1C, 4096)
                                 ).then_inc(SX1[tc % 2].h, 16)
                    if STAGE == "x1":
                        sy.dma_start(out=y[tc * 128:(tc + 1) * 128, :],
                                     in_=abf(LF_X1C, 4096)
                                     ).then_inc(SX1[tc % 2].h, 16)
                em("sync", fsx)
                SX1[tc % 2].inc(32 if STAGE == "x1" else 16)

                nmv = ln_chunk(abf(LF_X1C, 4096), ab(LF_LN2S, 2048), LF_TRASH2,
                               x_wait=(evV.h, adv),
                               norm_wait=(tpe.h, prev_tr[0]) if tc else None)
                ARB = AB.ap().rearrange("p (a r) -> p a r", r=2048)
                for k0 in range(0, KCN, 4):
                    transpose_block(
                        [ab(LF_LN2S + kc * 128, 128)
                         for kc in range(k0, k0 + 4)],
                        waits=[(nmc.h, nmv)] if k0 == 0 else (),
                        batch_dst=ARB[:, (SC_LN2F // 2048) + k0:
                                      (SC_LN2F // 2048) + k0 + 4,
                                      tc * 128:(tc + 1) * 128])
                prev_tr[0] = tpe.n
            LN2_ALL = tev.n
            if STAGE == "ln2":
                dbg_dump(SC_LN2F, [(tev.h, LN2_ALL)])
            if STAGE == "ffw":
                def fdw(g):
                    g.wait_ge(S["cc"].h, CC_ALL)
                    yf = y.rearrange("a d -> (a d)")
                    g.dma_start(out=yf[0:SHARD].rearrange("(a b) -> a b", b=2048),
                                in_=cc_out.ap()[0:1, :].rearrange(
                                    "a (r b) -> (a r) b", b=2048)
                                ).then_inc(S["yod"].h, 16)
                    g.dma_start(out=yf[SHARD:2 * SHARD].rearrange(
                                    "(a b) -> a b", b=2048),
                                in_=(cc_in if DUMP_CCIN else cc_out
                                     ).ap()[2:3, :].rearrange(
                                    "a (r b) -> (a r) b", b=2048)
                                ).then_inc(S["yod"].h, 16)
                    g.wait_ge(S["yod"].h, 32)
                em("gpsimd", fdw)
                S["yod"].inc(32)
                raise StopEmit()

            X1_ALL = [c.n for c in SX1]
            if STAGE == "x1":
                def f_fin(sy, xa=tuple(X1_ALL)):
                    for c, vv in zip(SX1, xa):
                        sy.wait_ge(c.h, vv)
                em("sync", f_fin)
            else:
                down_mm_prev_block = 0
                for blk_i in range(FBLK):
                    t0 = blk_i * FBT
                    evv_hlast = None
                    for ic in range(ICN):
                        gb = FF_GB + (ic % 2) * 4096
                        rel = MM_AFTER_WO if (blk_i == 0 and ic < 2) else None
                        mg = mm.n

                        ffg = FFG[ic % 2]
                        cc5 = cc_out.ap().rearrange(
                            "r (s i p e) -> r i p s e", s=2, i=ICN, p=128)

                        def fwl(sy, ic=ic, gb=gb, blk_i=blk_i, mg=mg, rel=rel,
                                ffg=ffg, cc5=cc5, nmb=lnslot[0]):
                            sy.wait_ge(S["cc"].h, CC_ALL)
                            if FFN_BARRIER and ic < 2:
                                sy.wait_ge(nmc.h, nmb)
                                sy.wait_ge(tev.h, LN2_ALL)
                            if FFN_SERIAL:
                                sy.wait_ge(mm.h, mg)
                            elif rel is not None:
                                sy.wait_ge(mm.h, rel)
                            elif blk_i > 0 or ic >= 2:
                                sy.wait_ge(mm.h, mg - 4)
                            for m in range(2):
                                for r in range(NCORES):
                                    src = cc5[3 * r + m:3 * r + m + 1,
                                              ic:ic + 1, :, :, :]
                                    dst = AB[:, gb + m * 2048 + 2 * r * 128:
                                             gb + m * 2048 + (2 * r + 2) * 128
                                             ].rearrange("p (s e) -> p s e",
                                                         e=128)
                                    sy.dma_start(out=dst, in_=src
                                                 ).then_inc(ffg.h, 16)
                        em("sync", fwl)
                        ffg.inc(16 * 16)
                        fwv = ffg.n
                        ginfo = [pa_alloc() for _ in range(4)]

                        def fg(te, ic=ic, gb=gb, t0=t0, fwv=fwv, ginfo=ginfo,
                               ln2_all=LN2_ALL, ffg=ffg):
                            te.wait_ge(ffg.h, fwv)
                            if ic == 0:
                                te.wait_ge(tev.h, ln2_all)
                            ins = None
                            for j in range(4):
                                mi, th = j // 2, j % 2
                                gi, slot, guard = ginfo[j]
                                pa = PA[:, slot * 512:(slot + 1) * 512]
                                if guard is not None:
                                    te.wait_ge(guard[0], guard[1])
                                for kc in range(KCN):
                                    ins = te.matmul(
                                        pa,
                                        AB[:, gb + mi * 2048 + kc * 128:
                                           gb + mi * 2048 + (kc + 1) * 128],
                                        ab(SC_LN2F + kc * 2048 + t0 + th * 512,
                                           512),
                                        start=(kc == 0), stop=(kc == KCN - 1))
                                ins.then_inc(mm.h, 1)
                        em("tensor", fg)
                        for j in range(4):
                            mm.inc()
                        mm_g = mm.n
                        for th in range(2):
                            sg = SE_SG + th * 512
                            gslot = ginfo[th][1]
                            uslot = ginfo[2 + th][1]
                            mmv_g = mm_g - 3 + th
                            mmv_u = mm_g - 1 + th

                            def fsg(sc, sg=sg, gslot=gslot, mmv=mmv_g):
                                sc.wait_ge(mm.h, mmv)
                                sc.activation(
                                    out=ab(sg, 512),
                                    in_=PA[:, gslot * 512:(gslot + 1) * 512],
                                    func=AF.Silu).then_inc(evA.h, 1)
                            em("scalar", fsg)
                            pa_free(gslot, evA)
                            ev = evA.n
                            dguard = down_mm_prev_block

                            def fh(v, sg=sg, uslot=uslot, ic=ic, th=th, ev=ev,
                                   mmv=mmv_u, dguard=dguard):
                                v.wait_ge(mm.h, mmv)
                                v.wait_ge(evA.h, ev)
                                if dguard:
                                    v.wait_ge(mm.h, dguard)
                                v.tensor_tensor(
                                    out=ab(FF_H + ic * 1024 + th * 512, 512),
                                    in0=PA[:, uslot * 512:(uslot + 1) * 512],
                                    in1=ab(sg, 512), op=ALU.mult
                                    ).then_inc(evV.h, 1)
                            em("vector", fh)
                            pa_free(uslot, evV)
                            evv_hlast = evV.n
                    for pr in range(4):
                        for fh_i in range(2):
                            gi_d = evb.n
                            for ic in range(ICN):
                                db = FF_DW + (ic % 2) * 1024
                                mgd = mm.n
                                g0r = ic * 128
                                r0 = g0r // IDXD
                                off = g0r - r0 * IDXD
                                take = min(128, IDXD - off)
                                pieces = [(r0, off, take)]
                                if take < 128:
                                    pieces.append((r0 + 1, 0, 128 - take))
                                first_wave = (blk_i == 0 and pr == 0 and
                                              fh_i == 0 and ic < 2)

                                ffd = FFD[ic % 2]

                                def fdl(sy, ic=ic, db=db, fh_i=fh_i, mgd=mgd,
                                        pieces=tuple(pieces), fwv0=first_wave,
                                        ffd=ffd):
                                    sy.wait_ge(S["cc"].h, CC_ALL)
                                    if fwv0:
                                        sy.wait_ge(mm.h, MM_AFTER_WO)
                                    elif ic < 2:
                                        sy.wait_ge(mm.h, mgd)
                                    else:
                                        sy.wait_ge(mm.h, mgd - 1)
                                    d0 = 0
                                    for (r, off2, tk) in pieces:
                                        src = cc_out.ap()[3 * r + 2:3 * r + 3,
                                                          :].rearrange(
                                            "a (i e) -> (a i) e", i=IDXD)[
                                            off2:off2 + tk,
                                            fh_i * 1024:(fh_i + 1) * 1024]
                                        sy.dma_start(out=AB[d0:d0 + tk,
                                                            db:db + 1024],
                                                     in_=src).then_inc(ffd.h, 16)
                                        d0 += tk
                                em("sync", fdl)
                                ffd.inc(16 * len(pieces))
                                fwv = ffd.n

                                def fdm(te, ic=ic, db=db, pr=pr, fwv=fwv,
                                        gi_d=gi_d, hv=evv_hlast, ffd=ffd):
                                    te.wait_ge(ffd.h, fwv)
                                    if ic == 0:
                                        te.wait_ge(evV.h, hv)
                                        te.wait_ge(evb.h, gi_d)
                                    ins = None
                                    for t2 in range(2):
                                        tc8 = pr * 2 + t2
                                        for nn2 in range(2):
                                            ins = te.matmul(
                                                PBp[:, (t2 * 2 + nn2) * 512:
                                                    (t2 * 2 + nn2 + 1) * 512],
                                                ab(FF_H + ic * 1024 + tc8 * 128,
                                                   128),
                                                AB[:, db + nn2 * 512:
                                                   db + (nn2 + 1) * 512],
                                                start=(ic == 0),
                                                stop=(ic == ICN - 1))
                                    ins.then_inc(mm.h, 1)
                                em("tensor", fdm)
                                mm.inc()
                            mmv = mm.n
                            for t2 in range(2):
                                tc8g = t0 // 128 + pr * 2 + t2
                                x1b = FF_X1 + (tc8g % 2) * 4096
                                x1l = X1L[tc8g % 2]
                                if fh_i == 0:
                                    evg = evb.n

                                    def fx1(sy, tc8g=tc8g, x1b=x1b, evg=evg,
                                            x1l=x1l):
                                        for c, vv in zip(SX1, X1_ALL):
                                            sy.wait_ge(c.h, vv)
                                        if evg > 0:
                                            sy.wait_ge(evb.h, evg)
                                        sy.dma_start(
                                            out=abf(x1b, 4096),
                                            in_=x1d[tc8g * 128:(tc8g + 1) * 128,
                                                    :]).then_inc(x1l.h, 16)
                                    em("sync", fx1)
                                    x1l.inc(16)
                                xrv = x1l.n
                                x2b = FF_X2 + (t2 % 2) * 2048
                                syo = SYO[t2 % 2]
                                yg = syo.n

                                def fev(v, t2=t2, x1b=x1b, x2b=x2b, mmv=mmv,
                                        xrv=xrv, fh_i=fh_i, yg=yg, x1l=x1l,
                                        syo=syo):
                                    v.wait_ge(mm.h, mmv)
                                    v.wait_ge(x1l.h, xrv)
                                    if yg >= 16:
                                        v.wait_ge(syo.h, yg)
                                    v.tensor_tensor(
                                        out=abf(x2b, 2048),
                                        in0=PBp[:, t2 * 1024:(t2 + 1) * 1024],
                                        in1=abf(x1b + fh_i * 2048, 2048),
                                        op=ALU.add).then_inc(evb.h, 1)
                                em("vector", fev)
                                evb.inc()
                                ev2 = evb.n

                                def fyo(sy, tc8g=tc8g, x2b=x2b, fh_i=fh_i,
                                        ev2=ev2, syo=syo):
                                    sy.wait_ge(evb.h, ev2)
                                    sy.dma_start(
                                        out=y[tc8g * 128:(tc8g + 1) * 128,
                                              fh_i * 1024:(fh_i + 1) * 1024],
                                        in_=abf(x2b, 2048)).then_inc(syo.h, 16)
                                em("sync", fyo)
                                syo.inc(16)
                    down_mm_prev_block = mm.n
                yo_tot = [c.n for c in SYO]

                def f_fin(sy, yt=tuple(yo_tot)):
                    for c, vv in zip(SYO, yt):
                        sy.wait_ge(c.h, vv)
                em("sync", f_fin)


        except StopEmit:
            pass

        blk = es.enter_context(nc.Block())

        @blk.sync
        def _(h):
            for f in Q["sync"]:
                f(h)

        @blk.tensor
        def _(h):
            for f in Q["tensor"]:
                f(h)

        @blk.vector
        def _(h):
            for f in Q["vector"]:
                f(h)

        @blk.scalar
        def _(h):
            for f in Q["scalar"]:
                f(h)

        @blk.gpsimd
        def _(h):
            for f in Q["gpsimd"]:
                f(h)

    return nc


def kernel(L0_attn_x, L0_attn_Wqk, L0_attn_bqk, L0_attn_Wv, L0_attn_bv,
           L0_attn_Wu, L0_attn_bu, L0_attn_Wo, L0_attn_bo, L0_attn_ln_g,
           L0_attn_ln_b, L0_ffn_lut, L0_ffn_gate_idx, L0_ffn_up_idx,
           L0_ffn_down_idx):
    x = np.ascontiguousarray(np.asarray(L0_attn_x, dtype=np.float32)
                             ).reshape(NCORES * TOK, D)
    base = dict(
        wqk=np.ascontiguousarray(np.asarray(L0_attn_Wqk, dtype=np.float32)),
        wv=np.ascontiguousarray(np.asarray(L0_attn_Wv, dtype=np.float32)),
        wu=np.ascontiguousarray(np.asarray(L0_attn_Wu, dtype=np.float32)),
        wo=np.ascontiguousarray(np.asarray(L0_attn_Wo, dtype=np.float32)),
        bqk=np.asarray(L0_attn_bqk, dtype=np.float32).reshape(1, D),
        bv=np.asarray(L0_attn_bv, dtype=np.float32).reshape(1, D),
        bu=np.asarray(L0_attn_bu, dtype=np.float32).reshape(1, D),
        bo=np.asarray(L0_attn_bo, dtype=np.float32).reshape(1, D),
        lng=np.asarray(L0_attn_ln_g, dtype=np.float32).reshape(1, D),
        lnb=np.asarray(L0_attn_ln_b, dtype=np.float32).reshape(1, D),
        lut=np.asarray(L0_ffn_lut, dtype=np.float32).reshape(1, 16),
    )
    g_idx = np.asarray(L0_ffn_gate_idx, dtype=np.int32)
    u_idx = np.asarray(L0_ffn_up_idx, dtype=np.int32)
    d_idx = np.asarray(L0_ffn_down_idx, dtype=np.int32)
    in_maps = []
    for c in range(NCORES):
        m = dict(base)
        m["x"] = np.ascontiguousarray(x[c * TOK:(c + 1) * TOK])
        m["gidx"] = np.ascontiguousarray(g_idx[:, c * IDXC:(c + 1) * IDXC])
        m["uidx"] = np.ascontiguousarray(u_idx[:, c * IDXC:(c + 1) * IDXC])
        m["didx"] = np.ascontiguousarray(d_idx[:, c * IDXD:(c + 1) * IDXD])
        in_maps.append(m)
    nc = build()
    res = run_bass_kernel_spmd(nc, in_maps, core_ids=list(range(NCORES)))
    out = np.concatenate([r["y"] for r in res.results], axis=0)
    return out.reshape(4, 4096, D)


# version 17
# speedup vs baseline: 1.0116x; 1.0116x over previous
import contextlib
import numpy as np
import concourse.bass as bass
import concourse.mybir as mybir
from concourse.bass_utils import run_bass_kernel_spmd

F32 = mybir.dt.float32
BF16 = mybir.dt.bfloat16
I32 = mybir.dt.int32
AF = mybir.ActivationFunctionType
ALU = mybir.AluOpType

D = 2048
TOK = 2048
NCH = 16
KCN = 16
HEADS = 16
INTER = 5504
ICN = 43
EPS = 1e-5
SWP = 8
SWN = D // SWP
IDXC = 256
IDXD = 688
NCORES = 8
FBLK = 2
FBT = TOK // FBLK

STAGE = "full"
NO_GATHER = False
GATHER_LIMIT = 10 ** 9
GATHER_NO_DMA = False
GATHER_LEVEL = 4
DUMP_CCIN = False
FFN_SERIAL = False
FFN_BARRIER = False
X1_SETTLE = False
A1_CHUNKS = NCH
ORDER = ["a1", "a2a", "a2b", "a2c", "a4a", "x1", "ln2", "ffw", "ffh", "ffl", "full"]

LF = 0
PBF = 32768
SC = 65536
SD = 98304
ABLEN = 102400
G_FIDX = SC
G_ACC0 = SC + 2112
G_ACC1 = SC + 2816
G_PLN = SC + 3520
G_STG = SC + 4224
SE_LN1 = SC + 4736
SE_TR1 = SC + 6784
SE_V = SC + 8832
SE_ELU = SC + 9344
SC_X = SC + 9856
SC_KVS = SC + 18048
WB_AB = SC + 22144
SC_U = SC
SC_LN2F = SC
SD_U = SD
FF_DW = SD
SE_SG = SD + 2048
LF_PFM = LF
LF_TBF = LF + 4096
LF_TRASH = LF + 16384
LF_TRASH2 = LF + 4096
LF_GB = LF + 8192
LF_XQ = LF + 12288
LF_X1Q = LF + 14336
LF_LN2S = LF + 16384
LF_X1C = LF + 18432
AT_BF = LF + 18432
WB_WO = LF + 22528
LF_KVB = LF + 30720
FF_H = 0
FF_GB = 45056
FF_X1 = 53248
FF_X2 = 61440


class Ctr:
    __slots__ = ("h", "n")

    def __init__(self, h):
        self.h = h
        self.n = 0

    def inc(self, k=1):
        self.n += k
        return self.n


def kvcol(h):
    return (h % 2) * 1024 + (h // 2) * 128


def build():
    nc = bass.Bass("TRN2", target_bir_lowering=False,
                   detect_race_conditions=False)
    dt = nc.dram_tensor
    x_in = dt("x", [TOK, D], F32, kind="ExternalInput").ap()
    wmats = {nm: dt(nm, [D, D], F32, kind="ExternalInput").ap()
             for nm in ("wqk", "wv", "wu", "wo")}
    vecs = {nm: dt(nm, [1, D], F32, kind="ExternalInput").ap()
            for nm in ("bqk", "bv", "bu", "bo", "lng", "lnb")}
    lut = dt("lut", [1, 16], F32, kind="ExternalInput").ap()
    gidx = dt("gidx", [INTER, IDXC], I32, kind="ExternalInput").ap()
    uidx = dt("uidx", [INTER, IDXC], I32, kind="ExternalInput").ap()
    didx = dt("didx", [D, IDXD], I32, kind="ExternalInput").ap()
    y = dt("y", [TOK, D], F32, kind="ExternalOutput").ap()
    x1d = dt("x1d", [TOK, D], F32, kind="Internal").ap()
    SHARD = 2 * ICN * 128 * 128
    cc_in = dt("cc_in", [3, SHARD], BF16, kind="Internal")
    cc_out = dt("cc_out", [3 * NCORES, SHARD], BF16, kind="Internal")
    scr_dbg = dt("scr_dbg", [128, 128], BF16, kind="Internal")
    kv_in = dt("kv_in", [128, D], F32, kind="Internal")
    kv_out = dt("kv_out", [128, D], F32, kind="Internal")

    es = contextlib.ExitStack()
    with es:
        AB = es.enter_context(nc.sbuf_tensor("AB", [128, ABLEN], BF16))
        biasT = es.enter_context(nc.sbuf_tensor("biasT", [128, D], BF16))
        ident = es.enter_context(nc.sbuf_tensor("ident", [128, 128], BF16))
        ones = es.enter_context(nc.sbuf_tensor("ones", [128, 128], BF16))
        lutbc = es.enter_context(nc.sbuf_tensor("lutbc", [128, 16], F32))
        stats = es.enter_context(nc.sbuf_tensor("stats", [128, 64], F32))
        PA = es.enter_context(nc.psum_tensor("PA", [128, 1024], F32))
        PBp = es.enter_context(nc.psum_tensor("PBp", [128, 2048], F32))
        PT = es.enter_context(nc.psum_tensor("PT", [128, 2048], BF16))

        sem_names = ["cst", "cstC", "cnv", "gac", "tpe", "tev", "mm",
                     "evA", "evV", "evb", "stA", "var", "ln", "nm", "fz",
                     "cc", "kv", "pd", "sx2", "sbo", "sgb", "yod", "evE"]
        S = {nm: Ctr(es.enter_context(nc.semaphore(nm))) for nm in sem_names}

        def slot_sems(name, k):
            return [Ctr(es.enter_context(nc.semaphore(f"{name}{i}")))
                    for i in range(k)]

        SX = slot_sems("sx", 2)
        SW = slot_sems("sw", 2)
        SIDX = slot_sems("sidx", 2)
        SSH = slot_sems("ssh", 4)
        FFG = slot_sems("ffg", 2)
        FFD = slot_sems("ffd", 2)
        XQ = slot_sems("xq", 4)
        X1L = slot_sems("x1l", 2)
        SX1 = slot_sems("sx1", 2)
        SYO = slot_sems("syo", 2)
        mm, evA, evV, evb = S["mm"], S["evA"], S["evV"], S["evb"]
        tpe, tev, nmc, pd = S["tpe"], S["tev"], S["nm"], S["pd"]

        Q = {e: [] for e in ("sync", "tensor", "vector", "scalar", "gpsimd")}

        def em(eng, fn):
            Q[eng].append(fn)

        def ab(off, ln):
            return AB[:, off:off + ln]

        def abf(off, ln):
            return AB[:, off:off + ln].bitcast(F32)

        cst = S["cst"]

        def g0(g):
            g.dma_start(out=biasT[0:1, 0:D], in_=vecs["bqk"][:, :]).then_inc(cst.h, 16)
            g.dma_start(out=biasT[32:33, 0:D], in_=vecs["bv"][:, :]).then_inc(cst.h, 16)
            g.dma_start(out=biasT[64:65, 0:D], in_=vecs["bu"][:, :]).then_inc(cst.h, 16)
            g.dma_start(out=lutbc[:], in_=lut.broadcast_to([128, 16])).then_inc(cst.h, 16)
            g.memset(ones[:], 1.0)
            g.memset(ident[:], 0.0).then_inc(S["cstC"].h, 1)
            g.wait_ge(cst.h, 16 * 4)
            g.wait_ge(S["cstC"].h, 1)
            g.affine_select(out=ident[:], in_=ident[:], compare_op=ALU.not_equal,
                            fill=1.0, base=0, pattern=[[-1, 128]],
                            channel_multiplier=1).then_inc(S["cstC"].h, 1)
        em("gpsimd", g0)
        cst.inc(16 * 4)
        S["cstC"].inc(2)
        CST = cst.n
        CSTC = S["cstC"].n

        cnv = S["cnv"]

        tg_n = [0]

        def transpose_block(srcs, waits=(), batch_dst=None, tile_evac=None):
            g = tg_n[0]
            tg_n[0] += 1
            col0 = (g % 2) * 1024
            t0 = tpe.n
            for i, s in enumerate(srcs):
                if isinstance(s, tuple):
                    s, rows = s
                else:
                    rows = 128

                def f(te, i=i, s=s, g=g, col0=col0,
                      waits=tuple(waits) if i == 0 else ()):
                    for (h, v) in waits:
                        te.wait_ge(h, v)
                    if i == 0 and g >= 2:
                        te.wait_ge(tev.h, g - 1)
                    if g == 0 and i == 0:
                        te.wait_ge(cst.h, CST)
                        te.wait_ge(S["cstC"].h, CSTC)
                    te.transpose(PT[:, col0 + i * 128:col0 + (i + 1) * 128],
                                 s, ident[:]).then_inc(tpe.h, 1)
                em("tensor", f)
                tpe.inc()
            tneed = tpe.n
            n = len(srcs)
            if batch_dst is not None:
                def fe(sc, bd=batch_dst, col0=col0, n=n, tneed=tneed):
                    sc.wait_ge(tpe.h, tneed)
                    sc.activation(out=bd, in_=PT[:, col0:col0 + n * 128],
                                  func=AF.Copy).then_inc(tev.h, 1)
                em("scalar", fe)
            else:
                def fe(sc, col0=col0, n=n, tneed=tneed, te_=tile_evac):
                    sc.wait_ge(tpe.h, tneed)
                    for i in range(n):
                        te_(sc, i, col0 + i * 128, i == n - 1)
                em("scalar", fe)
            tev.inc()
            return tev.n

        lnslot = [0]

        def ln_chunk(src_f32, dst_bf16, trash_off, x_wait=None,
                     emit_norm=True, norm_wait=None):
            cid = lnslot[0]
            lnslot[0] += 1
            s = cid % 8
            st = stats[:, 8 * s:8 * s + 8]
            stA, var, ln = S["stA"], S["var"], S["ln"]

            def fa(sc, src=src_f32, st=st, xw=x_wait, cid=cid, to=trash_off):
                if cid >= 8:
                    sc.wait_ge(nmc.h, cid - 7)
                if xw is not None:
                    sc.wait_ge(xw[0], xw[1])
                sc.activation(out=ab(to, 2048), in_=src, func=AF.Copy,
                              scale=1.0 / D, accum_out=st[:, 0:1])
                sc.activation(out=ab(to, 2048), in_=src, func=AF.Square,
                              scale=1.0 / float(np.sqrt(D)),
                              accum_out=st[:, 1:2])
                sc.activation(out=st[:, 2:3], in_=st[:, 0:1],
                              func=AF.Square).then_inc(stA.h, 1)
            em("scalar", fa)
            stA.inc()
            vA = stA.n

            def fp(g, st=st, vA=vA):
                g.wait_ge(stA.h, vA)
                g.tensor_scalar(out=st[:, 3:4], in0=st[:, 2:3], scalar1=-1.0,
                                scalar2=float(EPS), op0=ALU.mult,
                                op1=ALU.add).then_inc(var.h, 1)
            em("vector", fp)
            var.inc()
            vV = var.n

            def fb(sc, st=st, vV=vV):
                sc.wait_ge(var.h, vV)
                sc.activation(out=st[:, 4:5], in_=st[:, 1:2], func=AF.Ln,
                              bias=st[:, 3:4])
                sc.activation(out=st[:, 5:6], in_=st[:, 4:5], func=AF.Exp,
                              scale=-0.5).then_inc(ln.h, 1)
            em("scalar", fb)
            ln.inc()
            lV = ln.n
            if not emit_norm:
                return lV, st

            def fv(v, src=src_f32, dst=dst_bf16, st=st, lV=lV,
                   nw=norm_wait):
                v.wait_ge(ln.h, lV)
                if nw is not None:
                    v.wait_ge(nw[0], nw[1])
                v.tensor_scalar(out=dst, in0=src, scalar1=st[:, 0:1],
                                scalar2=st[:, 5:6], op0=ALU.subtract,
                                op1=ALU.mult)
                v.tensor_copy(out=stats[:, 56:60], in_=dst[:, 0:4]
                              ).then_inc(nmc.h, 1)
            em("vector", fv)
            nmc.inc()
            return nmc.n

        pa_group = [0]
        pa_guard = [None, None]

        def pa_alloc():
            gi = pa_group[0]
            pa_group[0] += 1
            slot = gi % 2
            return gi, slot, pa_guard[slot]

        def pa_free(slot, ctr):
            ctr.inc()
            pa_guard[slot] = (ctr.h, ctr.n)
            return ctr.n

        gac = S["gac"]
        units = []
        for m, src in ((0, gidx), (1, uidx)):
            for pair in range(22):
                o0 = 2 * pair
                n_oc = 2 if o0 + 1 < ICN else 1
                loads = [(src[(o0 + i) * 128:(o0 + i + 1) * 128, :],
                          i * IDXC, IDXC) for i in range(n_oc)]
                tiles = []
                for i in range(n_oc):
                    for sub in range(2):
                        tiles.append((i * IDXC + sub * 128, 128,
                                      ("tile", m, sub * ICN + (o0 + i))))
                units.append((loads, n_oc * IDXC, tiles))
        for oc in range(16):
            loads = [(didx[oc * 128:(oc + 1) * 128, :], 0, IDXD)]
            tiles = [(icl * 128, 128 if icl < 5 else 48,
                      ("down", icl * 128, oc * 128)) for icl in range(6)]
            units.append((loads, IDXD, tiles))
        NUNITS = len(units)
        g_next_load = [0]
        g_cmp = [0]
        g_tile = [0]
        g_acc_tpe = {}
        pend_tp = []
        unit_loadv = {}

        def emit_load(j):
            loads, fd, _ = units[j]
            sidx = SIDX[j % 2]

            def fg(g, loads=loads, j=j, sidx=sidx):
                if j >= 3:
                    g.wait_ge(gac.h, j - 2)
                fb2 = G_FIDX + (j % 3) * 704
                for (ap_, co, cl) in loads:
                    g.dma_start(out=AB[:, fb2 + co:fb2 + co + cl],
                                in_=ap_).then_inc(sidx.h, 16)
            em("gpsimd", fg)
            sidx.inc(16 * len(loads))
            unit_loadv[j] = sidx.n

        def emit_dummy_load(j):
            sidx = SIDX[j % 2]

            def fg(g, j=j, sidx=sidx):
                g.wait_ge(gac.h, j - 2)
                g.dma_start(out=AB[:, G_FIDX + (j % 3) * 704:
                                   G_FIDX + (j % 3) * 704 + 64],
                            in_=gidx[0:128, 0:64]).then_inc(sidx.h, 16)
            em("gpsimd", fg)
            sidx.inc(16)
            unit_loadv[j] = sidx.n

        def drain_cmp(k=1):
            if NO_GATHER:
                return
            for _ in range(k):
                if g_cmp[0] >= min(NUNITS, GATHER_LIMIT):
                    return
                ui = g_cmp[0]
                g_cmp[0] += 1
                while g_next_load[0] <= min(ui + 2, NUNITS + 1):
                    j = g_next_load[0]
                    if j < NUNITS:
                        emit_load(j)
                    else:
                        emit_dummy_load(j)
                    g_next_load[0] += 1
                loads, fd, tiles = units[ui]
                sidx = SIDX[ui % 2]
                iV = unit_loadv[ui + 2]
                accb = G_ACC0 if ui % 2 == 0 else G_ACC1
                if GATHER_LEVEL < 2:
                    continue

                def fv(v, ui=ui, fd=fd, iV=iV, sidx=sidx, accb=accb,
                       ptv=g_acc_tpe.get(ui - 2, 0)):
                    v.wait_ge(sidx.h, iV)
                    if ptv:
                        v.wait_ge(tpe.h, ptv)
                    if ui == 0:
                        v.wait_ge(cst.h, CST)
                    fb2 = G_FIDX + (ui % 3) * 704
                    acc = ab(accb, fd)
                    pln = ab(G_PLN, fd)
                    src = ab(fb2, fd)
                    for j in range(16):
                        dst = acc if j == 0 else pln
                        v.tensor_scalar(out=dst, in0=src, scalar1=float(j),
                                        scalar2=lutbc[:, j:j + 1],
                                        op0=ALU.is_equal, op1=ALU.mult)
                        if j > 0:
                            v.tensor_tensor(out=acc, in0=acc, in1=pln,
                                            op=ALU.add)
                    v.tensor_copy(out=ab(G_PLN, 8), in_=ab(accb, 8)
                                  ).then_inc(gac.h, 1)
                em("vector", fv)
                gac.inc()
                pend_tp.append((ui, gac.n, tiles, accb))

        def drain_tp(k=1, lag=2):
            if NO_GATHER:
                return
            for _ in range(k):
                if len(pend_tp) <= lag:
                    return
                ui, gv, tiles, accb = pend_tp.pop(0)
                first = True
                if GATHER_LEVEL < 3:
                    g_acc_tpe[ui] = tpe.n
                    continue
                for b0 in range(0, len(tiles), 4):
                    blk_tiles = tiles[b0:b0 + 4]
                    evinfo = []
                    for (scol, rows, dst) in blk_tiles:
                        gt = g_tile[0]
                        g_tile[0] += 1
                        ssh = SSH[gt % 4]
                        evinfo.append((G_STG + (gt % 4) * 128, rows, dst,
                                       gt, ssh, ssh.n))
                        ssh.inc(16)

                    def mk_te(evinfo):
                        def te_(sc, i, ptcol, is_last):
                            stg, rows, dst, gt, ssh, sshv = evinfo[i]
                            if gt >= 4:
                                sc.wait_ge(ssh.h, sshv)
                            ins = sc.activation(
                                out=AB[0:rows, stg:stg + 128],
                                in_=PT[0:rows, ptcol:ptcol + 128],
                                func=AF.Copy)
                            if is_last:
                                ins.then_inc(tev.h, 1)
                            if dst[0] == "down":
                                r0, c0 = dst[1], dst[2]
                                dd = cc_in.ap()[2:3, :].rearrange(
                                    "a (r c) -> (a r) c", r=IDXD)[
                                    r0:r0 + rows, c0:c0 + 128]
                            else:
                                _, mi, ti = dst
                                dd = cc_in.ap()[mi:mi + 1,
                                                ti * 16384:(ti + 1) * 16384
                                                ].rearrange(
                                    "a (p e) -> (a p) e", p=128)[0:rows, :]
                            if GATHER_NO_DMA:
                                sc.dma_start(out=scr_dbg.ap()[0:rows, :],
                                             in_=AB[0:rows, stg:stg + 128]
                                             ).then_inc(ssh.h, 16)
                            else:
                                sc.dma_start(out=dd,
                                             in_=AB[0:rows, stg:stg + 128]
                                             ).then_inc(ssh.h, 16)
                        return te_

                    srcs = [(ab(accb + scol, 128), rows)
                            for (scol, rows, dst) in blk_tiles]
                    if GATHER_LEVEL < 4:
                        def te_noop(sc, i, ptcol, is_last, n=len(srcs)):
                            if is_last:
                                sc.activation(out=ab(G_STG, 128),
                                              in_=PT[:, ptcol:ptcol + 128],
                                              func=AF.Copy).then_inc(tev.h, 1)
                        transpose_block(srcs,
                                        waits=[(gac.h, gv)] if first else (),
                                        tile_evac=te_noop)
                    else:
                        transpose_block(srcs,
                                        waits=[(gac.h, gv)] if first else (),
                                        tile_evac=mk_te(evinfo))
                    first = False
                g_acc_tpe[ui] = tpe.n

        stage_n = ORDER.index(STAGE)

        class StopEmit(Exception):
            pass

        def dbg_dump(region_off, wait_list, n_el=32768, f32=False):
            def fd(g, ro=region_off, wl=tuple(wait_list), n_el=n_el, f32=f32):
                for (h, v) in wl:
                    g.wait_ge(h, v)
                if f32:
                    na = n_el // 2 // 2048
                    yv = y.rearrange("(a p) d -> p a d", p=128)[:, 0:na, :]
                    sv = abf(ro, n_el).rearrange("p (a d) -> p a d", d=2048)
                else:
                    na = n_el // 2048
                    yv = y.rearrange("(a p) d -> p a d", p=128)[:, 0:na, :]
                    sv = ab(ro, n_el).rearrange("p (a d) -> p a d", d=2048)
                g.dma_start(out=yv, in_=sv).then_inc(S["yod"].h, 16)
                g.wait_ge(S["yod"].h, 16)
            em("gpsimd", fd)
            S["yod"].inc(16)
            raise StopEmit()

        try:
            sd_release = {}
            sweep_swv = {}

            def emit_sweep_load(wname, s, swn, wb_base, wb_sz,
                                extra_w_wait=None, wb_map=None,
                                rel_override=None):
                wb = (wb_map[s] if wb_map is not None
                      else wb_base + (s % 2) * wb_sz)
                rel = (rel_override if rel_override is not None
                       else sd_release.get(wb))
                sw = SW[s % 2]

                def fw(g, s=s, wb=wb, rel=rel, wname=wname, swn=swn,
                       wb_sz=wb_sz, eww=extra_w_wait, sw=sw):
                    if eww is not None and s < 2:
                        g.wait_ge(eww[0], eww[1])
                    if rel is not None:
                        g.wait_ge(rel[0], rel[1])
                    src = wmats[wname].rearrange("(kc p) d -> p kc d",
                                                 p=128)[
                        :, :, s * swn:(s + 1) * swn]
                    g.dma_start(out=AB[:, wb:wb + wb_sz].rearrange(
                        "p (kc n) -> p kc n", n=swn), in_=src
                        ).then_inc(sw.h, 16)
                    g.dma_start(out=AB[:, wb:wb + 64],
                                in_=src[:, 0, 0:64]).then_inc(sw.h, 16)
                em("gpsimd", fw)
                sw.inc(32)
                sweep_swv[(wname, s)] = sw.n


            emit_sweep_load("wqk", 0, SWN, WB_AB, 4096)
            emit_sweep_load("wqk", 1, SWN, WB_AB, 4096)

            lf_tev = []
            prev_tr = [0]
            for tc in range(A1_CHUNKS):
                xb = SC_X + (tc % 2) * 4096
                sx = SX[tc % 2]

                def fx(sy, tc=tc, xb=xb, sx=sx):
                    if tc >= 2:
                        sy.wait_ge(nmc.h, tc - 1)
                    sy.dma_start(out=abf(xb, 4096),
                                 in_=x_in[tc * 128:(tc + 1) * 128, :]
                                 ).then_inc(sx.h, 16)
                em("sync", fx)
                sx.inc(16)
                nmv = ln_chunk(abf(xb, 4096), ab(SE_LN1, 2048), SE_TR1,
                               x_wait=(sx.h, sx.n),
                               norm_wait=(tpe.h, prev_tr[0]) if tc else None)
                ARB = AB.ap().rearrange("p (a r) -> p a r", r=2048)
                for k0 in range(0, KCN, 4):
                    transpose_block(
                        [ab(SE_LN1 + kc * 128, 128)
                         for kc in range(k0, k0 + 4)],
                        waits=[(nmc.h, nmv)] if k0 == 0 else (),
                        batch_dst=ARB[:, (LF // 2048) + k0:
                                      (LF // 2048) + k0 + 4,
                                      tc * 128:(tc + 1) * 128])
                lf_tev.append(tev.n)
                prev_tr[0] = tpe.n
            if stage_n == 0:
                dbg_dump(LF, [(tev.h, tev.n)], n_el=A1_CHUNKS * 2048)

            def proj_sweeps(wname, bias_part, post, lhsT_off=LF, ready=None,
                            swp=SWP, wb_base=WB_AB, wb_sz=4096, extra_w_wait=None,
                            lhsT_tcmajor=False, s_order=None, wb_map=None):
                swn = D // swp
                pend = []
                for s in (range(swp) if s_order is None else s_order):
                    wb = (wb_map[s] if wb_map is not None
                          else wb_base + (s % 2) * wb_sz)
                    sw = SW[s % 2]
                    if (wname, s) not in sweep_swv:
                        emit_sweep_load(wname, s, swn, wb_base, wb_sz,
                                        extra_w_wait, wb_map=wb_map)
                    swv = sweep_swv[(wname, s)]
                    for tc in range(NCH):
                        while pend:
                            em("tensor", pend.pop(0))
                        gi, slot, guard = pa_alloc()
                        pa = PA[:, slot * 512:slot * 512 + swn]
                        rdy = ready[tc] if ready is not None else None

                        def fm(te, tc=tc, s=s, wb=wb, pa=pa, swv=swv,
                               guard=guard, bp=bias_part, lo=lhsT_off, rdy=rdy,
                               swn=swn, sw=sw, tcm=lhsT_tcmajor):
                            te.wait_ge(sw.h, swv)
                            if rdy is not None:
                                te.wait_ge(tev.h, rdy)
                            if guard is not None:
                                te.wait_ge(guard[0], guard[1])
                            te.matmul(pa, ones[bp:bp + 1, :],
                                      biasT[bp:bp + 1, s * swn:(s + 1) * swn],
                                      start=True, stop=False)
                            ins = None
                            for kc in range(KCN):
                                if tcm:
                                    lt = lo + tc * 2048 + kc * 128
                                else:
                                    lt = lo + kc * 2048 + tc * 128
                                ins = te.matmul(
                                    pa, ab(lt, 128),
                                    AB[:, wb + kc * swn:wb + (kc + 1) * swn],
                                    start=False, stop=(kc == KCN - 1))
                            ins.then_inc(mm.h, 1)
                        em("tensor", fm)
                        mm.inc()
                        d = post(tc, s, pa, mm.n, slot, swn)
                        if d is not None:
                            pend.append(d)
                    sd_release[wb] = (mm.h, mm.n)
                while pend:
                    em("tensor", pend.pop(0))

            pdv = {}

            def post_p(tc, s, pa, mmv, slot, swn):
                dst = ab(PBF + tc * 2048 + s * swn, swn)
                m = ab(SE_ELU, swn)
                r = ab(SE_ELU + swn, swn)

                def fv(v, pa=pa, mmv=mmv, m=m, r=r):
                    v.wait_ge(mm.h, mmv)
                    v.tensor_scalar(out=m, in0=pa, scalar1=0.0, scalar2=None,
                                    op0=ALU.min)
                    v.tensor_scalar(out=r, in0=pa, scalar1=0.0, scalar2=-1.0,
                                    op0=ALU.max, op1=ALU.add).then_inc(evV.h, 1)
                em("vector", fv)
                pa_free(slot, evV)

                def fs(sc, m=m, ev=evV.n):
                    sc.wait_ge(evV.h, ev)
                    sc.activation(out=m, in_=m, func=AF.Exp).then_inc(S["fz"].h, 1)
                em("scalar", fs)
                S["fz"].inc()
                fzv = S["fz"].n

                def fv2(v, dst=dst, m=m, r=r, fzv=fzv):
                    v.wait_ge(S["fz"].h, fzv)
                    v.tensor_tensor(out=dst, in0=r, in1=m,
                                    op=ALU.add).then_inc(pd.h, 1)
                em("vector", fv2)
                pd.inc()
                pdv[(tc, s)] = pd.n
                return None

            proj_sweeps("wqk", 0, post_p, ready=lf_tev)
            if stage_n == 1:
                dbg_dump(PBF, [(pd.h, pd.n)])
            PD_ALL = pd.n
            MM_AFTER_P = mm.n

            def f_bo(g, mg=MM_AFTER_P):
                g.wait_ge(mm.h, mg)
                g.dma_start(out=biasT[0:1, 0:D], in_=vecs["bo"][:, :]
                            ).then_inc(S["sbo"].h, 16)
            em("gpsimd", f_bo)
            S["sbo"].inc(16)
            CST_BO = S["sbo"].n

            def post_v(tc, s, pa, mmv, slot, swn):
                vb = SE_V + (tc % 2) * swn

                def fs(sc, pa=pa, mmv=mmv, vb=vb, swn=swn):
                    sc.wait_ge(mm.h, mmv)
                    sc.activation(out=ab(vb, swn), in_=pa,
                                  func=AF.Silu).then_inc(evA.h, 1)
                em("scalar", fs)
                pa_free(slot, evA)
                ev = evA.n
                last = (tc == NCH - 1 and s == SWP - 1)
                pdw = pdv[(tc, s)]

                def fk(te, tc=tc, s=s, vb=vb, ev=ev, last=last, pdw=pdw):
                    te.wait_ge(evA.h, ev)
                    te.wait_ge(pd.h, pdw)
                    ins = None
                    for h2 in range(2):
                        h = 2 * s + h2
                        ins = te.matmul(
                            PBp[:, kvcol(h):kvcol(h) + 128],
                            ab(PBF + tc * 2048 + s * swn + h2 * 128, 128),
                            AB[:, vb + h2 * 128:vb + h2 * 128 + 128],
                            start=(tc == 0), stop=(tc == NCH - 1))
                    if last:
                        ins.then_inc(S["kv"].h, 1)
                if tc % 4 == 3:
                    drain_tp(1, lag=1)
                    drain_cmp(1)
                return fk

            proj_sweeps("wv", 32, post_v, ready=lf_tev)
            S["kv"].inc()
            kv_pe_done = S["kv"].n


            def f_kvev(v):
                v.wait_ge(S["kv"].h, kv_pe_done)
                v.tensor_copy(out=abf(SC_KVS, 4096), in_=PBp[:]
                              ).then_inc(evb.h, 1)
            em("vector", f_kvev)
            evb.inc()
            evb_kv = evb.n

            def f_kvdma(sy):
                sy.wait_ge(evb.h, evb_kv)
                sy.dma_start(out=kv_in.ap()[:, :], in_=abf(SC_KVS, 4096)
                             ).then_inc(S["kv"].h, 16)
            em("sync", f_kvdma)
            S["kv"].inc(16)
            kv_dma_done = S["kv"].n
            if stage_n == 2:
                dbg_dump(SC_KVS, [(S["kv"].h, kv_dma_done)], n_el=4096,
                         f32=True)

            guardv = {}

            def post_u(tc, s, pa, mmv, slot, swn):
                dst = ab(SC_U + s * 2048 + tc * swn, swn)
                need_g = (s <= 2)
                need_kv = (8 <= s <= 10)

                def fs(sc, pa=pa, mmv=mmv, dst=dst, need_g=need_g,
                       need_kv=need_kv):
                    sc.wait_ge(mm.h, mmv)
                    if need_g:
                        sc.wait_ge(gac.h, guardv["gac"])
                        sc.wait_ge(tpe.h, guardv["tpe"])
                        for c, v in zip(SSH, guardv["ssh"]):
                            sc.wait_ge(c.h, v)
                    if need_kv:
                        sc.wait_ge(S["kv"].h, kv_dma_done)
                    sc.activation(out=dst, in_=pa, func=AF.Silu
                                  ).then_inc(evA.h, 1)
                em("scalar", fs)
                pa_free(slot, evA)
                if s >= 4 and tc in (3, 11):
                    drain_tp(1, lag=1)
                    drain_cmp(1)
                if s == 3 and tc == NCH - 1:
                    while g_cmp[0] < min(NUNITS, GATHER_LIMIT):
                        drain_tp(1, lag=1)
                        drain_cmp(1)
                    drain_tp(NUNITS + 2, lag=0)
                    guardv["gac"] = gac.n
                    guardv["tpe"] = tpe.n
                    guardv["ssh"] = [c.n for c in SSH]
                return None

            proj_sweeps("wu", 64, post_u, ready=lf_tev,
                            swp=16, wb_base=SD_U, wb_sz=2048,
                            s_order=range(15, -1, -1))
            MM_AFTER_U = mm.n
            if stage_n == 3:
                dbg_dump(SC_U, [(evA.h, evA.n)])

            WO_WB = {0: WB_WO, 1: WB_WO + 4096, 2: LF_TBF, 3: LF_XQ,
                     4: SD_U, 5: WB_WO, 6: WB_WO + 4096, 7: LF_TBF}

            def f_ccr(g):
                g.wait_ge(S["kv"].h, kv_dma_done)
                g.collective_compute(
                    "AllReduce", ALU.add,
                    replica_groups=[[0, 1], [2, 3], [4, 5], [6, 7]],
                    ins=[kv_in.ap().opt()], outs=[kv_out.ap().opt()],
                ).then_inc(S["cc"].h, 1)
            em("gpsimd", f_ccr)
            S["cc"].inc(1)

            def f_gb(g, mg=MM_AFTER_U):
                g.wait_ge(mm.h, mg)
                g.wait_ge(S["cc"].h, 1)
                g.dma_start(out=ab(LF_KVB, 2048), in_=kv_out.ap()[:, :]
                            ).then_inc(S["sgb"].h, 16)
                g.dma_start(out=ab(LF_GB, 2048),
                            in_=vecs["lng"].broadcast_to([128, D])
                            ).then_inc(S["sgb"].h, 16)
                g.dma_start(out=ab(LF_GB + 2048, 2048),
                            in_=vecs["lnb"].broadcast_to([128, D])
                            ).then_inc(S["sgb"].h, 16)
            em("gpsimd", f_gb)
            S["sgb"].inc(48)
            CST2 = S["sgb"].n

            for s_ in range(4):
                emit_sweep_load("wo", s_, SWN, WB_WO, 4096,
                                extra_w_wait=(mm.h, MM_AFTER_U), wb_map=WO_WB)
            emit_sweep_load("wo", 4, SWN, WB_WO, 4096, wb_map=WO_WB,
                            rel_override=(mm.h, MM_AFTER_U))

            def f_cca(g):
                for c, v in zip(SSH, guardv["ssh"]):
                    g.wait_ge(c.h, v)
                g.wait_ge(gac.h, guardv["gac"])
                g.collective_compute(
                    "AllGather", ALU.bypass,
                    replica_groups=[[0, 1, 2, 3, 4, 5, 6, 7]],
                    ins=[cc_in.ap().opt()], outs=[cc_out.ap().opt()],
                ).then_inc(S["cc"].h, 1)
            em("gpsimd", f_cca)
            S["cc"].inc(1)
            CC_ALL = S["cc"].n

            fzc = S["fz"]

            def f_pdall(te):
                te.wait_ge(pd.h, PD_ALL)
            em("tensor", f_pdall)
            pend_tu = []
            uln_tev = {}
            ffz_tc = {}
            for tc in range(NCH):
                while len(pend_tu) > 2:
                    pend_tu.pop(0)()
                pf = LF_PFM + (tc % 2) * 2048
                for k0 in range(0, KCN, 4):
                    transpose_block(
                        [ab(PBF + tc * 2048 + kc * 128, 128)
                         for kc in range(k0, k0 + 4)],
                        batch_dst=ab(pf + k0 * 128, 512))
                tev_need = tev.n
                gi_e = S["evE"].n

                def fe(te, tc=tc, pf=pf, tev_need=tev_need, gi_e=gi_e):
                    te.wait_ge(tev.h, tev_need)
                    te.wait_ge(S["sgb"].h, CST2)
                    if tc >= 1:
                        te.wait_ge(S["evE"].h, gi_e)
                    ins = None
                    for h in range(HEADS):
                        ins = te.matmul(
                            PBp[:, h * 128:(h + 1) * 128],
                            ab(pf + h * 128, 128),
                            AB[:, LF_KVB + kvcol(h):LF_KVB + kvcol(h) + 128],
                            start=True, stop=True)
                    ins.then_inc(mm.h, 1)
                em("tensor", fe)
                mm.inc()
                mmv = mm.n
                lV, st = ln_chunk(PBp[:], None, LF_TRASH, x_wait=(mm.h, mmv),
                                  emit_norm=False)
                atb = ab(AT_BF + (tc % 2) * 2048, 2048)
                ffz_prev = ffz_tc[tc - 2] if tc >= 2 else 0

                def fev(sc, atb=atb, fp=ffz_prev):
                    if fp:
                        sc.wait_ge(fzc.h, fp)
                    sc.activation(out=atb, in_=PBp[:], func=AF.Copy
                                  ).then_inc(S["evE"].h, 1)
                em("scalar", fev)
                S["evE"].inc()
                ub = ab(PBF + tc * 2048, 2048)

                def ff(v, tc=tc, st=st, lV=lV, ub=ub, atb=atb,
                       eE=S["evE"].n):
                    v.wait_ge(S["ln"].h, lV)
                    v.wait_ge(S["evE"].h, eE)
                    v.tensor_scalar(out=ub, in0=atb, scalar1=st[:, 0:1],
                                    scalar2=st[:, 5:6], op0=ALU.subtract,
                                    op1=ALU.mult).then_inc(evb.h, 1)
                    v.tensor_tensor(out=ub, in0=ub, in1=ab(LF_GB, 2048),
                                    op=ALU.mult)
                    v.tensor_tensor(out=ub, in0=ub, in1=ab(LF_GB + 2048, 2048),
                                    op=ALU.add)
                    AB3 = AB.ap().rearrange("p (s x) -> p s x", x=2048)
                    ub3 = AB.ap().rearrange("p (a x) -> p a x", x=128)
                    ubo = PBF + tc * 2048
                    v.tensor_tensor(
                        out=ub3[:, ubo // 128:ubo // 128 + 16, :],
                        in0=ub3[:, ubo // 128:ubo // 128 + 16, :],
                        in1=AB3[:, SC_U // 2048:SC_U // 2048 + 16,
                                tc * 128:(tc + 1) * 128],
                        op=ALU.mult)
                    v.tensor_copy(out=stats[:, 56:60], in_=ub[0:128, 0:4]
                                  ).then_inc(fzc.h, 1)
                em("vector", ff)
                evb.inc()
                fzc.inc()
                fzv = fzc.n
                ffz_tc[tc] = fzv

                def fnm(v):
                    v.engine_nop().then_inc(nmc.h, 1)
                em("vector", fnm)
                nmc.inc()

                def mk_tu(tc=tc, fzv=fzv):
                    def go():
                        for k0 in range(0, KCN, 4):
                            transpose_block(
                                [ab(PBF + tc * 2048 + kc * 128, 128)
                                 for kc in range(k0, k0 + 4)],
                                waits=[(fzc.h, fzv)] if k0 == 0 else (),
                                batch_dst=ab(PBF + tc * 2048 + k0 * 128, 512))
                        uln_tev[tc] = tev.n
                    return go
                pend_tu.append(mk_tu())
            for fcb in pend_tu:
                fcb()
            pend_tu = []
            uln_all = tev.n
            if stage_n == 4:
                dbg_dump(PBF, [(tev.h, uln_all)])

            M0 = mm.n
            for s_, relv in ((5, 16), (6, 32), (7, 48)):
                emit_sweep_load("wo", s_, SWN, WB_WO, 4096, wb_map=WO_WB,
                                rel_override=(mm.h, M0 + relv))

            def post_o(tc, s, pa, mmv, slot, swn):
                def fv(v, pa=pa, mmv=mmv, tc=tc, s=s):
                    v.wait_ge(mm.h, mmv)
                    AB3 = AB.ap().rearrange("p (k x) -> p k x", x=2048)
                    pa3 = pa.rearrange("p (i x) -> p i x", x=128)
                    v.tensor_copy(
                        out=AB3[:, SC // 2048 + 2 * s:SC // 2048 + 2 * s + 2,
                                tc * 128:(tc + 1) * 128],
                        in_=pa3).then_inc(evV.h, 1)
                em("vector", fv)
                pa_free(slot, evV)
                return None

            def f_wog(te):
                te.wait_ge(S["sbo"].h, CST_BO)
            em("tensor", f_wog)
            proj_sweeps("wo", 0, post_o, lhsT_off=PBF,
                        ready=[uln_tev[t] for t in range(NCH)],
                        wb_base=WB_WO, extra_w_wait=(mm.h, MM_AFTER_U),
                        lhsT_tcmajor=True, wb_map=WO_WB)
            EV_WO = evV.n
            MM_AFTER_WO = mm.n

            ABF3 = abf(0, ABLEN).rearrange("p (k x) -> p k x", x=128)
            X1K = (LF_X1C // 2) // 128
            for tc in range(NCH):
                nm_prev = nmc.n

                def fl(sy, tc=tc, nm_prev=nm_prev,
                       stv=SX1[(tc - 1) % 2].n):
                    if tc == 0:
                        sy.wait_ge(fzc.h, ffz_tc[NCH - 1])
                    if tc >= 1:
                        sy.wait_ge(nmc.h, nm_prev)
                        sy.wait_ge(SX1[(tc - 1) % 2].h, stv)
                    sy.dma_start(out=abf(LF_X1C, 4096),
                                 in_=x_in[tc * 128:(tc + 1) * 128, :]
                                 ).then_inc(S["sx2"].h, 16)
                em("sync", fl)
                S["sx2"].inc(16)
                xwv = S["sx2"].n

                def fadd(v, tc=tc, xwv=xwv):
                    v.wait_ge(S["sx2"].h, xwv)
                    v.wait_ge(evV.h, EV_WO)
                    AB3 = AB.ap().rearrange("p (k x) -> p k x", x=2048)
                    v.tensor_tensor(
                        out=ABF3[:, X1K:X1K + 16, :],
                        in0=ABF3[:, X1K:X1K + 16, :],
                        in1=AB3[:, SC // 2048:SC // 2048 + 16,
                                tc * 128:(tc + 1) * 128],
                        op=ALU.add).then_inc(evV.h, 1)
                em("vector", fadd)
                evV.inc()
                adv = evV.n

                def fsx(sy, tc=tc, adv=adv):
                    sy.wait_ge(evV.h, adv)
                    sy.dma_start(out=x1d[tc * 128:(tc + 1) * 128, :],
                                 in_=abf(LF_X1C, 4096)
                                 ).then_inc(SX1[tc % 2].h, 16)
                    if STAGE == "x1":
                        sy.dma_start(out=y[tc * 128:(tc + 1) * 128, :],
                                     in_=abf(LF_X1C, 4096)
                                     ).then_inc(SX1[tc % 2].h, 16)
                em("sync", fsx)
                SX1[tc % 2].inc(32 if STAGE == "x1" else 16)

                nmv = ln_chunk(abf(LF_X1C, 4096), ab(LF_LN2S, 2048), LF_TRASH2,
                               x_wait=(evV.h, adv),
                               norm_wait=(tpe.h, prev_tr[0]) if tc else None)
                ARB = AB.ap().rearrange("p (a r) -> p a r", r=2048)
                for k0 in range(0, KCN, 4):
                    transpose_block(
                        [ab(LF_LN2S + kc * 128, 128)
                         for kc in range(k0, k0 + 4)],
                        waits=[(nmc.h, nmv)] if k0 == 0 else (),
                        batch_dst=ARB[:, (SC_LN2F // 2048) + k0:
                                      (SC_LN2F // 2048) + k0 + 4,
                                      tc * 128:(tc + 1) * 128])
                prev_tr[0] = tpe.n
            LN2_ALL = tev.n
            if STAGE == "ln2":
                dbg_dump(SC_LN2F, [(tev.h, LN2_ALL)])
            if STAGE == "ffw":
                def fdw(g):
                    g.wait_ge(S["cc"].h, CC_ALL)
                    yf = y.rearrange("a d -> (a d)")
                    g.dma_start(out=yf[0:SHARD].rearrange("(a b) -> a b", b=2048),
                                in_=cc_out.ap()[0:1, :].rearrange(
                                    "a (r b) -> (a r) b", b=2048)
                                ).then_inc(S["yod"].h, 16)
                    g.dma_start(out=yf[SHARD:2 * SHARD].rearrange(
                                    "(a b) -> a b", b=2048),
                                in_=(cc_in if DUMP_CCIN else cc_out
                                     ).ap()[2:3, :].rearrange(
                                    "a (r b) -> (a r) b", b=2048)
                                ).then_inc(S["yod"].h, 16)
                    g.wait_ge(S["yod"].h, 32)
                em("gpsimd", fdw)
                S["yod"].inc(32)
                raise StopEmit()

            X1_ALL = [c.n for c in SX1]
            if STAGE == "x1":
                def f_fin(sy, xa=tuple(X1_ALL)):
                    for c, vv in zip(SX1, xa):
                        sy.wait_ge(c.h, vv)
                em("sync", f_fin)
            else:
                down_mm_prev_block = 0
                for blk_i in range(FBLK):
                    t0 = blk_i * FBT
                    evv_hlast = None
                    for ic in range(ICN):
                        gb = FF_GB + (ic % 2) * 4096
                        rel = MM_AFTER_WO if (blk_i == 0 and ic < 2) else None
                        mg = mm.n

                        ffg = FFG[ic % 2]
                        cc6 = cc_out.ap().rearrange(
                            "(r t) (s i p e) -> t s i p r e",
                            t=3, s=2, i=ICN, p=128)

                        def fwl(sy, ic=ic, gb=gb, blk_i=blk_i, mg=mg, rel=rel,
                                ffg=ffg, cc6=cc6, nmb=lnslot[0]):
                            sy.wait_ge(S["cc"].h, CC_ALL)
                            if FFN_BARRIER and ic < 2:
                                sy.wait_ge(nmc.h, nmb)
                                sy.wait_ge(tev.h, LN2_ALL)
                            if FFN_SERIAL:
                                sy.wait_ge(mm.h, mg)
                            elif rel is not None:
                                sy.wait_ge(mm.h, rel)
                            elif blk_i > 0 or ic >= 2:
                                sy.wait_ge(mm.h, mg - 4)
                            for m in range(2):
                                V = AB[:, gb + m * 2048:
                                       gb + (m + 1) * 2048].rearrange(
                                    "p (k e) -> p k e", e=128)
                                for s2 in range(2):
                                    sy.dma_start(out=V[:, s2::2, :],
                                                 in_=cc6[m, s2, ic]
                                                 ).then_inc(ffg.h, 16)
                        em("sync", fwl)
                        ffg.inc(16 * 4)
                        fwv = ffg.n
                        ginfo = [pa_alloc() for _ in range(4)]

                        def fg(te, ic=ic, gb=gb, t0=t0, fwv=fwv, ginfo=ginfo,
                               ln2_all=LN2_ALL, ffg=ffg):
                            te.wait_ge(ffg.h, fwv)
                            if ic == 0:
                                te.wait_ge(tev.h, ln2_all)
                            ins = None
                            for j in range(4):
                                mi, th = j // 2, j % 2
                                gi, slot, guard = ginfo[j]
                                pa = PA[:, slot * 512:(slot + 1) * 512]
                                if guard is not None:
                                    te.wait_ge(guard[0], guard[1])
                                for kc in range(KCN):
                                    ins = te.matmul(
                                        pa,
                                        AB[:, gb + mi * 2048 + kc * 128:
                                           gb + mi * 2048 + (kc + 1) * 128],
                                        ab(SC_LN2F + kc * 2048 + t0 + th * 512,
                                           512),
                                        start=(kc == 0), stop=(kc == KCN - 1))
                                ins.then_inc(mm.h, 1)
                        em("tensor", fg)
                        for j in range(4):
                            mm.inc()
                        mm_g = mm.n
                        for th in range(2):
                            sg = SE_SG + th * 512
                            gslot = ginfo[th][1]
                            uslot = ginfo[2 + th][1]
                            mmv_g = mm_g - 3 + th
                            mmv_u = mm_g - 1 + th

                            def fsg(sc, sg=sg, gslot=gslot, mmv=mmv_g):
                                sc.wait_ge(mm.h, mmv)
                                sc.activation(
                                    out=ab(sg, 512),
                                    in_=PA[:, gslot * 512:(gslot + 1) * 512],
                                    func=AF.Silu).then_inc(evA.h, 1)
                            em("scalar", fsg)
                            pa_free(gslot, evA)
                            ev = evA.n
                            dguard = down_mm_prev_block

                            def fh(v, sg=sg, uslot=uslot, ic=ic, th=th, ev=ev,
                                   mmv=mmv_u, dguard=dguard):
                                v.wait_ge(mm.h, mmv)
                                v.wait_ge(evA.h, ev)
                                if dguard:
                                    v.wait_ge(mm.h, dguard)
                                v.tensor_tensor(
                                    out=ab(FF_H + ic * 1024 + th * 512, 512),
                                    in0=PA[:, uslot * 512:(uslot + 1) * 512],
                                    in1=ab(sg, 512), op=ALU.mult
                                    ).then_inc(evV.h, 1)
                            em("vector", fh)
                            pa_free(uslot, evV)
                            evv_hlast = evV.n
                    for pr in range(4):
                        for fh_i in range(2):
                            gi_d = evb.n
                            for ic in range(ICN):
                                db = FF_DW + (ic % 2) * 1024
                                mgd = mm.n
                                g0r = ic * 128
                                r0 = g0r // IDXD
                                off = g0r - r0 * IDXD
                                take = min(128, IDXD - off)
                                pieces = [(r0, off, take)]
                                if take < 128:
                                    pieces.append((r0 + 1, 0, 128 - take))
                                first_wave = (blk_i == 0 and pr == 0 and
                                              fh_i == 0 and ic < 2)

                                ffd = FFD[ic % 2]

                                def fdl(sy, ic=ic, db=db, fh_i=fh_i, mgd=mgd,
                                        pieces=tuple(pieces), fwv0=first_wave,
                                        ffd=ffd):
                                    sy.wait_ge(S["cc"].h, CC_ALL)
                                    if fwv0:
                                        sy.wait_ge(mm.h, MM_AFTER_WO)
                                    elif ic < 2:
                                        sy.wait_ge(mm.h, mgd)
                                    else:
                                        sy.wait_ge(mm.h, mgd - 1)
                                    d0 = 0
                                    for (r, off2, tk) in pieces:
                                        src = cc_out.ap()[3 * r + 2:3 * r + 3,
                                                          :].rearrange(
                                            "a (i e) -> (a i) e", i=IDXD)[
                                            off2:off2 + tk,
                                            fh_i * 1024:(fh_i + 1) * 1024]
                                        sy.dma_start(out=AB[d0:d0 + tk,
                                                            db:db + 1024],
                                                     in_=src).then_inc(ffd.h, 16)
                                        d0 += tk
                                em("sync", fdl)
                                ffd.inc(16 * len(pieces))
                                fwv = ffd.n

                                def fdm(te, ic=ic, db=db, pr=pr, fwv=fwv,
                                        gi_d=gi_d, hv=evv_hlast, ffd=ffd):
                                    te.wait_ge(ffd.h, fwv)
                                    if ic == 0:
                                        te.wait_ge(evV.h, hv)
                                        te.wait_ge(evb.h, gi_d)
                                    ins = None
                                    for t2 in range(2):
                                        tc8 = pr * 2 + t2
                                        for nn2 in range(2):
                                            ins = te.matmul(
                                                PBp[:, (t2 * 2 + nn2) * 512:
                                                    (t2 * 2 + nn2 + 1) * 512],
                                                ab(FF_H + ic * 1024 + tc8 * 128,
                                                   128),
                                                AB[:, db + nn2 * 512:
                                                   db + (nn2 + 1) * 512],
                                                start=(ic == 0),
                                                stop=(ic == ICN - 1))
                                    ins.then_inc(mm.h, 1)
                                em("tensor", fdm)
                                mm.inc()
                            mmv = mm.n
                            for t2 in range(2):
                                tc8g = t0 // 128 + pr * 2 + t2
                                x1b = FF_X1 + (tc8g % 2) * 4096
                                x1l = X1L[tc8g % 2]
                                if fh_i == 0:
                                    evg = evb.n

                                    def fx1(sy, tc8g=tc8g, x1b=x1b, evg=evg,
                                            x1l=x1l):
                                        for c, vv in zip(SX1, X1_ALL):
                                            sy.wait_ge(c.h, vv)
                                        if evg > 0:
                                            sy.wait_ge(evb.h, evg)
                                        sy.dma_start(
                                            out=abf(x1b, 4096),
                                            in_=x1d[tc8g * 128:(tc8g + 1) * 128,
                                                    :]).then_inc(x1l.h, 16)
                                    em("sync", fx1)
                                    x1l.inc(16)
                                xrv = x1l.n
                                x2b = FF_X2 + (t2 % 2) * 2048
                                syo = SYO[t2 % 2]
                                yg = syo.n

                                def fev(v, t2=t2, x1b=x1b, x2b=x2b, mmv=mmv,
                                        xrv=xrv, fh_i=fh_i, yg=yg, x1l=x1l,
                                        syo=syo):
                                    v.wait_ge(mm.h, mmv)
                                    v.wait_ge(x1l.h, xrv)
                                    if yg >= 16:
                                        v.wait_ge(syo.h, yg)
                                    v.tensor_tensor(
                                        out=abf(x2b, 2048),
                                        in0=PBp[:, t2 * 1024:(t2 + 1) * 1024],
                                        in1=abf(x1b + fh_i * 2048, 2048),
                                        op=ALU.add).then_inc(evb.h, 1)
                                em("vector", fev)
                                evb.inc()
                                ev2 = evb.n

                                def fyo(sy, tc8g=tc8g, x2b=x2b, fh_i=fh_i,
                                        ev2=ev2, syo=syo):
                                    sy.wait_ge(evb.h, ev2)
                                    sy.dma_start(
                                        out=y[tc8g * 128:(tc8g + 1) * 128,
                                              fh_i * 1024:(fh_i + 1) * 1024],
                                        in_=abf(x2b, 2048)).then_inc(syo.h, 16)
                                em("sync", fyo)
                                syo.inc(16)
                    down_mm_prev_block = mm.n
                yo_tot = [c.n for c in SYO]

                def f_fin(sy, yt=tuple(yo_tot)):
                    for c, vv in zip(SYO, yt):
                        sy.wait_ge(c.h, vv)
                em("sync", f_fin)


        except StopEmit:
            pass

        blk = es.enter_context(nc.Block())

        @blk.sync
        def _(h):
            for f in Q["sync"]:
                f(h)

        @blk.tensor
        def _(h):
            for f in Q["tensor"]:
                f(h)

        @blk.vector
        def _(h):
            for f in Q["vector"]:
                f(h)

        @blk.scalar
        def _(h):
            for f in Q["scalar"]:
                f(h)

        @blk.gpsimd
        def _(h):
            for f in Q["gpsimd"]:
                f(h)

    return nc


def kernel(L0_attn_x, L0_attn_Wqk, L0_attn_bqk, L0_attn_Wv, L0_attn_bv,
           L0_attn_Wu, L0_attn_bu, L0_attn_Wo, L0_attn_bo, L0_attn_ln_g,
           L0_attn_ln_b, L0_ffn_lut, L0_ffn_gate_idx, L0_ffn_up_idx,
           L0_ffn_down_idx):
    x = np.ascontiguousarray(np.asarray(L0_attn_x, dtype=np.float32)
                             ).reshape(NCORES * TOK, D)
    base = dict(
        wqk=np.ascontiguousarray(np.asarray(L0_attn_Wqk, dtype=np.float32)),
        wv=np.ascontiguousarray(np.asarray(L0_attn_Wv, dtype=np.float32)),
        wu=np.ascontiguousarray(np.asarray(L0_attn_Wu, dtype=np.float32)),
        wo=np.ascontiguousarray(np.asarray(L0_attn_Wo, dtype=np.float32)),
        bqk=np.asarray(L0_attn_bqk, dtype=np.float32).reshape(1, D),
        bv=np.asarray(L0_attn_bv, dtype=np.float32).reshape(1, D),
        bu=np.asarray(L0_attn_bu, dtype=np.float32).reshape(1, D),
        bo=np.asarray(L0_attn_bo, dtype=np.float32).reshape(1, D),
        lng=np.asarray(L0_attn_ln_g, dtype=np.float32).reshape(1, D),
        lnb=np.asarray(L0_attn_ln_b, dtype=np.float32).reshape(1, D),
        lut=np.asarray(L0_ffn_lut, dtype=np.float32).reshape(1, 16),
    )
    g_idx = np.asarray(L0_ffn_gate_idx, dtype=np.int32)
    u_idx = np.asarray(L0_ffn_up_idx, dtype=np.int32)
    d_idx = np.asarray(L0_ffn_down_idx, dtype=np.int32)
    in_maps = []
    for c in range(NCORES):
        m = dict(base)
        m["x"] = np.ascontiguousarray(x[c * TOK:(c + 1) * TOK])
        m["gidx"] = np.ascontiguousarray(g_idx[:, c * IDXC:(c + 1) * IDXC])
        m["uidx"] = np.ascontiguousarray(u_idx[:, c * IDXC:(c + 1) * IDXC])
        m["didx"] = np.ascontiguousarray(d_idx[:, c * IDXD:(c + 1) * IDXD])
        in_maps.append(m)
    nc = build()
    res = run_bass_kernel_spmd(nc, in_maps, core_ids=list(range(NCORES)))
    out = np.concatenate([r["y"] for r in res.results], axis=0)
    return out.reshape(4, 4096, D)


# version 18
# speedup vs baseline: 1.1599x; 1.1466x over previous
import contextlib
import numpy as np
import concourse.bass as bass
import concourse.mybir as mybir
from concourse.bass_utils import run_bass_kernel_spmd

F32 = mybir.dt.float32
BF16 = mybir.dt.bfloat16
I32 = mybir.dt.int32
AF = mybir.ActivationFunctionType
ALU = mybir.AluOpType

D = 2048
TOK = 2048
NCH = 16
KCN = 16
HEADS = 16
INTER = 5504
ICN = 43
EPS = 1e-5
SWP = 8
SWN = D // SWP
IDXC = 256
IDXD = 688
NCORES = 8
FBLK = 2
FBT = TOK // FBLK

STAGE = "full"
NO_GATHER = False
GATHER_LIMIT = 10 ** 9
GATHER_NO_DMA = False
GATHER_LEVEL = 4
DUMP_CCIN = False
FFN_SERIAL = False
FFN_BARRIER = False
X1_SETTLE = False
A1_CHUNKS = NCH
ORDER = ["a1", "a2a", "a2b", "a2c", "a4a", "x1", "ln2", "ffw", "ffh", "ffl", "full"]

LF = 0
PBF = 32768
SC = 65536
SD = 98304
ABLEN = 102400
G_FIDX = SC
G_ACC0 = SC + 2112
G_ACC1 = SC + 2816
G_PLN = SC + 3520
G_STG = SC + 4224
SE_LN1 = SC + 4736
SE_TR1 = SC + 6784
SE_V = SC + 8832
SE_ELU = SC + 9344
SC_X = SC + 9856
SC_KVS = SC + 18048
WB_AB = SC + 22144
SC_U = SC
SC_LN2F = SC
SD_U = SD
FF_DW = SD
SE_SG = 44032
LF_PFM = LF
LF_TBF = LF + 4096
LF_TRASH = LF + 16384
LF_TRASH2 = LF + 4096
LF_GB = LF + 8192
LF_XQ = LF + 12288
LF_X1Q = LF + 14336
LF_LN2S = LF + 16384
LF_X1C = LF + 18432
AT_BF = LF + 18432
WB_WO = LF + 22528
LF_KVB = LF + 30720
FF_H = 0
FF_GB = 45056
FF_X1 = 53248
FF_X2 = 61440


class Ctr:
    __slots__ = ("h", "n")

    def __init__(self, h):
        self.h = h
        self.n = 0

    def inc(self, k=1):
        self.n += k
        return self.n


def kvcol(h):
    return (h % 2) * 1024 + (h // 2) * 128


def build():
    nc = bass.Bass("TRN2", target_bir_lowering=False,
                   detect_race_conditions=False)
    dt = nc.dram_tensor
    x_in = dt("x", [TOK, D], F32, kind="ExternalInput").ap()
    wmats = {nm: dt(nm, [D, D], F32, kind="ExternalInput").ap()
             for nm in ("wqk", "wv", "wu", "wo")}
    vecs = {nm: dt(nm, [1, D], F32, kind="ExternalInput").ap()
            for nm in ("bqk", "bv", "bu", "bo", "lng", "lnb")}
    lut = dt("lut", [1, 16], F32, kind="ExternalInput").ap()
    gidx = dt("gidx", [INTER, IDXC], I32, kind="ExternalInput").ap()
    uidx = dt("uidx", [INTER, IDXC], I32, kind="ExternalInput").ap()
    didx = dt("didx", [D, IDXD], I32, kind="ExternalInput").ap()
    y = dt("y", [TOK, D], F32, kind="ExternalOutput").ap()
    x1d = dt("x1d", [TOK, D], F32, kind="Internal").ap()
    SHARD = 2 * ICN * 128 * 128
    cc_in = dt("cc_in", [3, SHARD], BF16, kind="Internal")
    cc_out = dt("cc_out", [3 * NCORES, SHARD], BF16, kind="Internal")
    scr_dbg = dt("scr_dbg", [128, 128], BF16, kind="Internal")
    kv_in = dt("kv_in", [128, D], F32, kind="Internal")
    kv_out = dt("kv_out", [128, D], F32, kind="Internal")

    es = contextlib.ExitStack()
    with es:
        AB = es.enter_context(nc.sbuf_tensor("AB", [128, ABLEN], BF16))
        biasT = es.enter_context(nc.sbuf_tensor("biasT", [128, D], BF16))
        ident = es.enter_context(nc.sbuf_tensor("ident", [128, 128], BF16))
        ones = es.enter_context(nc.sbuf_tensor("ones", [128, 128], BF16))
        lutbc = es.enter_context(nc.sbuf_tensor("lutbc", [128, 16], F32))
        stats = es.enter_context(nc.sbuf_tensor("stats", [128, 64], F32))
        PA = es.enter_context(nc.psum_tensor("PA", [128, 1024], F32))
        PBp = es.enter_context(nc.psum_tensor("PBp", [128, 2048], F32))
        PT = es.enter_context(nc.psum_tensor("PT", [128, 2048], BF16))

        sem_names = ["cst", "cstC", "cnv", "gac", "tpe", "tev", "mm",
                     "evA", "evV", "evb", "stA", "var", "ln", "nm", "fz",
                     "cc", "kv", "pd", "sx2", "sbo", "sgb", "yod", "evE"]
        S = {nm: Ctr(es.enter_context(nc.semaphore(nm))) for nm in sem_names}

        def slot_sems(name, k):
            return [Ctr(es.enter_context(nc.semaphore(f"{name}{i}")))
                    for i in range(k)]

        SX = slot_sems("sx", 2)
        SW = slot_sems("sw", 2)
        SIDX = slot_sems("sidx", 2)
        SSH = slot_sems("ssh", 4)
        FFG = slot_sems("ffg", 2)
        FFD = slot_sems("ffd", 4)
        XQ = slot_sems("xq", 4)
        X1L = slot_sems("x1l", 2)
        SX1 = slot_sems("sx1", 2)
        SYO = slot_sems("syo", 2)
        mm, evA, evV, evb = S["mm"], S["evA"], S["evV"], S["evb"]
        tpe, tev, nmc, pd = S["tpe"], S["tev"], S["nm"], S["pd"]

        Q = {e: [] for e in ("sync", "tensor", "vector", "scalar", "gpsimd")}

        def em(eng, fn):
            Q[eng].append(fn)

        def ab(off, ln):
            return AB[:, off:off + ln]

        def abf(off, ln):
            return AB[:, off:off + ln].bitcast(F32)

        cst = S["cst"]

        def g0(g):
            g.dma_start(out=biasT[0:1, 0:D], in_=vecs["bqk"][:, :]).then_inc(cst.h, 16)
            g.dma_start(out=biasT[32:33, 0:D], in_=vecs["bv"][:, :]).then_inc(cst.h, 16)
            g.dma_start(out=biasT[64:65, 0:D], in_=vecs["bu"][:, :]).then_inc(cst.h, 16)
            g.dma_start(out=lutbc[:], in_=lut.broadcast_to([128, 16])).then_inc(cst.h, 16)
            g.memset(ones[:], 1.0)
            g.memset(ident[:], 0.0).then_inc(S["cstC"].h, 1)
            g.wait_ge(cst.h, 16 * 4)
            g.wait_ge(S["cstC"].h, 1)
            g.affine_select(out=ident[:], in_=ident[:], compare_op=ALU.not_equal,
                            fill=1.0, base=0, pattern=[[-1, 128]],
                            channel_multiplier=1).then_inc(S["cstC"].h, 1)
        em("gpsimd", g0)
        cst.inc(16 * 4)
        S["cstC"].inc(2)
        CST = cst.n
        CSTC = S["cstC"].n

        cnv = S["cnv"]

        tg_n = [0]

        def transpose_block(srcs, waits=(), batch_dst=None, tile_evac=None):
            g = tg_n[0]
            tg_n[0] += 1
            col0 = (g % 2) * 1024
            t0 = tpe.n
            for i, s in enumerate(srcs):
                if isinstance(s, tuple):
                    s, rows = s
                else:
                    rows = 128

                def f(te, i=i, s=s, g=g, col0=col0,
                      waits=tuple(waits) if i == 0 else ()):
                    for (h, v) in waits:
                        te.wait_ge(h, v)
                    if i == 0 and g >= 2:
                        te.wait_ge(tev.h, g - 1)
                    if g == 0 and i == 0:
                        te.wait_ge(cst.h, CST)
                        te.wait_ge(S["cstC"].h, CSTC)
                    te.transpose(PT[:, col0 + i * 128:col0 + (i + 1) * 128],
                                 s, ident[:]).then_inc(tpe.h, 1)
                em("tensor", f)
                tpe.inc()
            tneed = tpe.n
            n = len(srcs)
            if batch_dst is not None:
                def fe(sc, bd=batch_dst, col0=col0, n=n, tneed=tneed):
                    sc.wait_ge(tpe.h, tneed)
                    sc.activation(out=bd, in_=PT[:, col0:col0 + n * 128],
                                  func=AF.Copy).then_inc(tev.h, 1)
                em("scalar", fe)
            else:
                def fe(sc, col0=col0, n=n, tneed=tneed, te_=tile_evac):
                    sc.wait_ge(tpe.h, tneed)
                    for i in range(n):
                        te_(sc, i, col0 + i * 128, i == n - 1)
                em("scalar", fe)
            tev.inc()
            return tev.n

        lnslot = [0]

        def ln_chunk(src_f32, dst_bf16, trash_off, x_wait=None,
                     emit_norm=True, norm_wait=None):
            cid = lnslot[0]
            lnslot[0] += 1
            s = cid % 8
            st = stats[:, 8 * s:8 * s + 8]
            stA, var, ln = S["stA"], S["var"], S["ln"]

            def fa(sc, src=src_f32, st=st, xw=x_wait, cid=cid, to=trash_off):
                if cid >= 8:
                    sc.wait_ge(nmc.h, cid - 7)
                if xw is not None:
                    sc.wait_ge(xw[0], xw[1])
                sc.activation(out=ab(to, 2048), in_=src, func=AF.Copy,
                              scale=1.0 / D, accum_out=st[:, 0:1])
                sc.activation(out=ab(to, 2048), in_=src, func=AF.Square,
                              scale=1.0 / float(np.sqrt(D)),
                              accum_out=st[:, 1:2])
                sc.activation(out=st[:, 2:3], in_=st[:, 0:1],
                              func=AF.Square).then_inc(stA.h, 1)
            em("scalar", fa)
            stA.inc()
            vA = stA.n

            def fp(g, st=st, vA=vA):
                g.wait_ge(stA.h, vA)
                g.tensor_scalar(out=st[:, 3:4], in0=st[:, 2:3], scalar1=-1.0,
                                scalar2=float(EPS), op0=ALU.mult,
                                op1=ALU.add).then_inc(var.h, 1)
            em("vector", fp)
            var.inc()
            vV = var.n

            def fb(sc, st=st, vV=vV):
                sc.wait_ge(var.h, vV)
                sc.activation(out=st[:, 4:5], in_=st[:, 1:2], func=AF.Ln,
                              bias=st[:, 3:4])
                sc.activation(out=st[:, 5:6], in_=st[:, 4:5], func=AF.Exp,
                              scale=-0.5).then_inc(ln.h, 1)
            em("scalar", fb)
            ln.inc()
            lV = ln.n
            if not emit_norm:
                return lV, st

            def fv(v, src=src_f32, dst=dst_bf16, st=st, lV=lV,
                   nw=norm_wait):
                v.wait_ge(ln.h, lV)
                if nw is not None:
                    v.wait_ge(nw[0], nw[1])
                v.tensor_scalar(out=dst, in0=src, scalar1=st[:, 0:1],
                                scalar2=st[:, 5:6], op0=ALU.subtract,
                                op1=ALU.mult)
                v.tensor_copy(out=stats[:, 56:60], in_=dst[:, 0:4]
                              ).then_inc(nmc.h, 1)
            em("vector", fv)
            nmc.inc()
            return nmc.n

        pa_group = [0]
        pa_guard = [None, None]

        def pa_alloc():
            gi = pa_group[0]
            pa_group[0] += 1
            slot = gi % 2
            return gi, slot, pa_guard[slot]

        def pa_free(slot, ctr):
            ctr.inc()
            pa_guard[slot] = (ctr.h, ctr.n)
            return ctr.n

        gac = S["gac"]
        units = []
        for m, src in ((0, gidx), (1, uidx)):
            for pair in range(22):
                o0 = 2 * pair
                n_oc = 2 if o0 + 1 < ICN else 1
                loads = [(src[(o0 + i) * 128:(o0 + i + 1) * 128, :],
                          i * IDXC, IDXC) for i in range(n_oc)]
                tiles = []
                for i in range(n_oc):
                    for sub in range(2):
                        tiles.append((i * IDXC + sub * 128, 128,
                                      ("tile", m, sub * ICN + (o0 + i))))
                units.append((loads, n_oc * IDXC, tiles))
        for oc in range(16):
            loads = [(didx[oc * 128:(oc + 1) * 128, :], 0, IDXD)]
            tiles = [(icl * 128, 128 if icl < 5 else 48,
                      ("down", icl * 128, oc * 128)) for icl in range(6)]
            units.append((loads, IDXD, tiles))
        NUNITS = len(units)
        g_next_load = [0]
        g_cmp = [0]
        g_tile = [0]
        g_acc_tpe = {}
        pend_tp = []
        unit_loadv = {}

        def emit_load(j):
            loads, fd, _ = units[j]
            sidx = SIDX[j % 2]

            def fg(g, loads=loads, j=j, sidx=sidx):
                if j >= 3:
                    g.wait_ge(gac.h, j - 2)
                fb2 = G_FIDX + (j % 3) * 704
                for (ap_, co, cl) in loads:
                    g.dma_start(out=AB[:, fb2 + co:fb2 + co + cl],
                                in_=ap_).then_inc(sidx.h, 16)
            em("gpsimd", fg)
            sidx.inc(16 * len(loads))
            unit_loadv[j] = sidx.n

        def emit_dummy_load(j):
            sidx = SIDX[j % 2]

            def fg(g, j=j, sidx=sidx):
                g.wait_ge(gac.h, j - 2)
                g.dma_start(out=AB[:, G_FIDX + (j % 3) * 704:
                                   G_FIDX + (j % 3) * 704 + 64],
                            in_=gidx[0:128, 0:64]).then_inc(sidx.h, 16)
            em("gpsimd", fg)
            sidx.inc(16)
            unit_loadv[j] = sidx.n

        def drain_cmp(k=1):
            if NO_GATHER:
                return
            for _ in range(k):
                if g_cmp[0] >= min(NUNITS, GATHER_LIMIT):
                    return
                ui = g_cmp[0]
                g_cmp[0] += 1
                while g_next_load[0] <= min(ui + 2, NUNITS + 1):
                    j = g_next_load[0]
                    if j < NUNITS:
                        emit_load(j)
                    else:
                        emit_dummy_load(j)
                    g_next_load[0] += 1
                loads, fd, tiles = units[ui]
                sidx = SIDX[ui % 2]
                iV = unit_loadv[ui + 2]
                accb = G_ACC0 if ui % 2 == 0 else G_ACC1
                if GATHER_LEVEL < 2:
                    continue

                def fv(v, ui=ui, fd=fd, iV=iV, sidx=sidx, accb=accb,
                       ptv=g_acc_tpe.get(ui - 2, 0)):
                    v.wait_ge(sidx.h, iV)
                    if ptv:
                        v.wait_ge(tpe.h, ptv)
                    if ui == 0:
                        v.wait_ge(cst.h, CST)
                    fb2 = G_FIDX + (ui % 3) * 704
                    acc = ab(accb, fd)
                    pln = ab(G_PLN, fd)
                    src = ab(fb2, fd)
                    for j in range(16):
                        dst = acc if j == 0 else pln
                        v.tensor_scalar(out=dst, in0=src, scalar1=float(j),
                                        scalar2=lutbc[:, j:j + 1],
                                        op0=ALU.is_equal, op1=ALU.mult)
                        if j > 0:
                            v.tensor_tensor(out=acc, in0=acc, in1=pln,
                                            op=ALU.add)
                    v.tensor_copy(out=ab(G_PLN, 8), in_=ab(accb, 8)
                                  ).then_inc(gac.h, 1)
                em("vector", fv)
                gac.inc()
                pend_tp.append((ui, gac.n, tiles, accb))

        def drain_tp(k=1, lag=2):
            if NO_GATHER:
                return
            for _ in range(k):
                if len(pend_tp) <= lag:
                    return
                ui, gv, tiles, accb = pend_tp.pop(0)
                first = True
                if GATHER_LEVEL < 3:
                    g_acc_tpe[ui] = tpe.n
                    continue
                for b0 in range(0, len(tiles), 4):
                    blk_tiles = tiles[b0:b0 + 4]
                    evinfo = []
                    for (scol, rows, dst) in blk_tiles:
                        gt = g_tile[0]
                        g_tile[0] += 1
                        ssh = SSH[gt % 4]
                        evinfo.append((G_STG + (gt % 4) * 128, rows, dst,
                                       gt, ssh, ssh.n))
                        ssh.inc(16)

                    def mk_te(evinfo):
                        def te_(sc, i, ptcol, is_last):
                            stg, rows, dst, gt, ssh, sshv = evinfo[i]
                            if gt >= 4:
                                sc.wait_ge(ssh.h, sshv)
                            ins = sc.activation(
                                out=AB[0:rows, stg:stg + 128],
                                in_=PT[0:rows, ptcol:ptcol + 128],
                                func=AF.Copy)
                            if is_last:
                                ins.then_inc(tev.h, 1)
                            if dst[0] == "down":
                                r0, c0 = dst[1], dst[2]
                                dd = cc_in.ap()[2:3, :].rearrange(
                                    "a (r c) -> (a r) c", r=IDXD)[
                                    r0:r0 + rows, c0:c0 + 128]
                            else:
                                _, mi, ti = dst
                                dd = cc_in.ap()[mi:mi + 1,
                                                ti * 16384:(ti + 1) * 16384
                                                ].rearrange(
                                    "a (p e) -> (a p) e", p=128)[0:rows, :]
                            if GATHER_NO_DMA:
                                sc.dma_start(out=scr_dbg.ap()[0:rows, :],
                                             in_=AB[0:rows, stg:stg + 128]
                                             ).then_inc(ssh.h, 16)
                            else:
                                sc.dma_start(out=dd,
                                             in_=AB[0:rows, stg:stg + 128]
                                             ).then_inc(ssh.h, 16)
                        return te_

                    srcs = [(ab(accb + scol, 128), rows)
                            for (scol, rows, dst) in blk_tiles]
                    if GATHER_LEVEL < 4:
                        def te_noop(sc, i, ptcol, is_last, n=len(srcs)):
                            if is_last:
                                sc.activation(out=ab(G_STG, 128),
                                              in_=PT[:, ptcol:ptcol + 128],
                                              func=AF.Copy).then_inc(tev.h, 1)
                        transpose_block(srcs,
                                        waits=[(gac.h, gv)] if first else (),
                                        tile_evac=te_noop)
                    else:
                        transpose_block(srcs,
                                        waits=[(gac.h, gv)] if first else (),
                                        tile_evac=mk_te(evinfo))
                    first = False
                g_acc_tpe[ui] = tpe.n

        stage_n = ORDER.index(STAGE)

        class StopEmit(Exception):
            pass

        def dbg_dump(region_off, wait_list, n_el=32768, f32=False):
            def fd(g, ro=region_off, wl=tuple(wait_list), n_el=n_el, f32=f32):
                for (h, v) in wl:
                    g.wait_ge(h, v)
                if f32:
                    na = n_el // 2 // 2048
                    yv = y.rearrange("(a p) d -> p a d", p=128)[:, 0:na, :]
                    sv = abf(ro, n_el).rearrange("p (a d) -> p a d", d=2048)
                else:
                    na = n_el // 2048
                    yv = y.rearrange("(a p) d -> p a d", p=128)[:, 0:na, :]
                    sv = ab(ro, n_el).rearrange("p (a d) -> p a d", d=2048)
                g.dma_start(out=yv, in_=sv).then_inc(S["yod"].h, 16)
                g.wait_ge(S["yod"].h, 16)
            em("gpsimd", fd)
            S["yod"].inc(16)
            raise StopEmit()

        try:
            sd_release = {}
            sweep_swv = {}

            def emit_sweep_load(wname, s, swn, wb_base, wb_sz,
                                extra_w_wait=None, wb_map=None,
                                rel_override=None):
                wb = (wb_map[s] if wb_map is not None
                      else wb_base + (s % 2) * wb_sz)
                rel = (rel_override if rel_override is not None
                       else sd_release.get(wb))
                sw = SW[s % 2]

                def fw(g, s=s, wb=wb, rel=rel, wname=wname, swn=swn,
                       wb_sz=wb_sz, eww=extra_w_wait, sw=sw):
                    if eww is not None and s < 2:
                        g.wait_ge(eww[0], eww[1])
                    if rel is not None:
                        g.wait_ge(rel[0], rel[1])
                    src = wmats[wname].rearrange("(kc p) d -> p kc d",
                                                 p=128)[
                        :, :, s * swn:(s + 1) * swn]
                    g.dma_start(out=AB[:, wb:wb + wb_sz].rearrange(
                        "p (kc n) -> p kc n", n=swn), in_=src
                        ).then_inc(sw.h, 16)
                    g.dma_start(out=AB[:, wb:wb + 64],
                                in_=src[:, 0, 0:64]).then_inc(sw.h, 16)
                em("gpsimd", fw)
                sw.inc(32)
                sweep_swv[(wname, s)] = sw.n


            emit_sweep_load("wqk", 0, SWN, WB_AB, 4096)
            emit_sweep_load("wqk", 1, SWN, WB_AB, 4096)

            lf_tev = []
            prev_tr = [0]
            for tc in range(A1_CHUNKS):
                xb = SC_X + (tc % 2) * 4096
                sx = SX[tc % 2]

                def fx(sy, tc=tc, xb=xb, sx=sx):
                    if tc >= 2:
                        sy.wait_ge(nmc.h, tc - 1)
                    sy.dma_start(out=abf(xb, 4096),
                                 in_=x_in[tc * 128:(tc + 1) * 128, :]
                                 ).then_inc(sx.h, 16)
                em("sync", fx)
                sx.inc(16)
                nmv = ln_chunk(abf(xb, 4096), ab(SE_LN1, 2048), SE_TR1,
                               x_wait=(sx.h, sx.n),
                               norm_wait=(tpe.h, prev_tr[0]) if tc else None)
                ARB = AB.ap().rearrange("p (a r) -> p a r", r=2048)
                for k0 in range(0, KCN, 4):
                    transpose_block(
                        [ab(SE_LN1 + kc * 128, 128)
                         for kc in range(k0, k0 + 4)],
                        waits=[(nmc.h, nmv)] if k0 == 0 else (),
                        batch_dst=ARB[:, (LF // 2048) + k0:
                                      (LF // 2048) + k0 + 4,
                                      tc * 128:(tc + 1) * 128])
                lf_tev.append(tev.n)
                prev_tr[0] = tpe.n
            if stage_n == 0:
                dbg_dump(LF, [(tev.h, tev.n)], n_el=A1_CHUNKS * 2048)

            def proj_sweeps(wname, bias_part, post, lhsT_off=LF, ready=None,
                            swp=SWP, wb_base=WB_AB, wb_sz=4096, extra_w_wait=None,
                            lhsT_tcmajor=False, s_order=None, wb_map=None):
                swn = D // swp
                pend = []
                for s in (range(swp) if s_order is None else s_order):
                    wb = (wb_map[s] if wb_map is not None
                          else wb_base + (s % 2) * wb_sz)
                    sw = SW[s % 2]
                    if (wname, s) not in sweep_swv:
                        emit_sweep_load(wname, s, swn, wb_base, wb_sz,
                                        extra_w_wait, wb_map=wb_map)
                    swv = sweep_swv[(wname, s)]
                    for tc in range(NCH):
                        while pend:
                            em("tensor", pend.pop(0))
                        gi, slot, guard = pa_alloc()
                        pa = PA[:, slot * 512:slot * 512 + swn]
                        rdy = ready[tc] if ready is not None else None

                        def fm(te, tc=tc, s=s, wb=wb, pa=pa, swv=swv,
                               guard=guard, bp=bias_part, lo=lhsT_off, rdy=rdy,
                               swn=swn, sw=sw, tcm=lhsT_tcmajor):
                            te.wait_ge(sw.h, swv)
                            if rdy is not None:
                                te.wait_ge(tev.h, rdy)
                            if guard is not None:
                                te.wait_ge(guard[0], guard[1])
                            te.matmul(pa, ones[bp:bp + 1, :],
                                      biasT[bp:bp + 1, s * swn:(s + 1) * swn],
                                      start=True, stop=False)
                            ins = None
                            for kc in range(KCN):
                                if tcm:
                                    lt = lo + tc * 2048 + kc * 128
                                else:
                                    lt = lo + kc * 2048 + tc * 128
                                ins = te.matmul(
                                    pa, ab(lt, 128),
                                    AB[:, wb + kc * swn:wb + (kc + 1) * swn],
                                    start=False, stop=(kc == KCN - 1))
                            ins.then_inc(mm.h, 1)
                        em("tensor", fm)
                        mm.inc()
                        d = post(tc, s, pa, mm.n, slot, swn)
                        if d is not None:
                            pend.append(d)
                    sd_release[wb] = (mm.h, mm.n)
                while pend:
                    em("tensor", pend.pop(0))

            pdv = {}

            def post_p(tc, s, pa, mmv, slot, swn):
                dst = ab(PBF + tc * 2048 + s * swn, swn)
                m = ab(SE_ELU, swn)
                r = ab(SE_ELU + swn, swn)

                def fv(v, pa=pa, mmv=mmv, m=m, r=r):
                    v.wait_ge(mm.h, mmv)
                    v.tensor_scalar(out=m, in0=pa, scalar1=0.0, scalar2=None,
                                    op0=ALU.min)
                    v.tensor_scalar(out=r, in0=pa, scalar1=0.0, scalar2=-1.0,
                                    op0=ALU.max, op1=ALU.add).then_inc(evV.h, 1)
                em("vector", fv)
                pa_free(slot, evV)

                def fs(sc, m=m, ev=evV.n):
                    sc.wait_ge(evV.h, ev)
                    sc.activation(out=m, in_=m, func=AF.Exp).then_inc(S["fz"].h, 1)
                em("scalar", fs)
                S["fz"].inc()
                fzv = S["fz"].n

                def fv2(v, dst=dst, m=m, r=r, fzv=fzv):
                    v.wait_ge(S["fz"].h, fzv)
                    v.tensor_tensor(out=dst, in0=r, in1=m,
                                    op=ALU.add).then_inc(pd.h, 1)
                em("vector", fv2)
                pd.inc()
                pdv[(tc, s)] = pd.n
                return None

            proj_sweeps("wqk", 0, post_p, ready=lf_tev)
            if stage_n == 1:
                dbg_dump(PBF, [(pd.h, pd.n)])
            PD_ALL = pd.n
            MM_AFTER_P = mm.n

            def f_bo(g, mg=MM_AFTER_P):
                g.wait_ge(mm.h, mg)
                g.dma_start(out=biasT[0:1, 0:D], in_=vecs["bo"][:, :]
                            ).then_inc(S["sbo"].h, 16)
            em("gpsimd", f_bo)
            S["sbo"].inc(16)
            CST_BO = S["sbo"].n

            def post_v(tc, s, pa, mmv, slot, swn):
                vb = SE_V + (tc % 2) * swn

                def fs(sc, pa=pa, mmv=mmv, vb=vb, swn=swn):
                    sc.wait_ge(mm.h, mmv)
                    sc.activation(out=ab(vb, swn), in_=pa,
                                  func=AF.Silu).then_inc(evA.h, 1)
                em("scalar", fs)
                pa_free(slot, evA)
                ev = evA.n
                last = (tc == NCH - 1 and s == SWP - 1)
                pdw = pdv[(tc, s)]

                def fk(te, tc=tc, s=s, vb=vb, ev=ev, last=last, pdw=pdw):
                    te.wait_ge(evA.h, ev)
                    te.wait_ge(pd.h, pdw)
                    ins = None
                    for h2 in range(2):
                        h = 2 * s + h2
                        ins = te.matmul(
                            PBp[:, kvcol(h):kvcol(h) + 128],
                            ab(PBF + tc * 2048 + s * swn + h2 * 128, 128),
                            AB[:, vb + h2 * 128:vb + h2 * 128 + 128],
                            start=(tc == 0), stop=(tc == NCH - 1))
                    if last:
                        ins.then_inc(S["kv"].h, 1)
                if tc % 4 == 3:
                    drain_tp(1, lag=1)
                    drain_cmp(1)
                return fk

            proj_sweeps("wv", 32, post_v, ready=lf_tev)
            S["kv"].inc()
            kv_pe_done = S["kv"].n


            def f_kvev(v):
                v.wait_ge(S["kv"].h, kv_pe_done)
                v.tensor_copy(out=abf(SC_KVS, 4096), in_=PBp[:]
                              ).then_inc(evb.h, 1)
            em("vector", f_kvev)
            evb.inc()
            evb_kv = evb.n

            def f_kvdma(sy):
                sy.wait_ge(evb.h, evb_kv)
                sy.dma_start(out=kv_in.ap()[:, :], in_=abf(SC_KVS, 4096)
                             ).then_inc(S["kv"].h, 16)
            em("sync", f_kvdma)
            S["kv"].inc(16)
            kv_dma_done = S["kv"].n
            if stage_n == 2:
                dbg_dump(SC_KVS, [(S["kv"].h, kv_dma_done)], n_el=4096,
                         f32=True)

            guardv = {}

            def post_u(tc, s, pa, mmv, slot, swn):
                dst = ab(SC_U + s * 2048 + tc * swn, swn)
                need_g = (s <= 2)
                need_kv = (8 <= s <= 10)

                def fs(sc, pa=pa, mmv=mmv, dst=dst, need_g=need_g,
                       need_kv=need_kv):
                    sc.wait_ge(mm.h, mmv)
                    if need_g:
                        sc.wait_ge(gac.h, guardv["gac"])
                        sc.wait_ge(tpe.h, guardv["tpe"])
                        for c, v in zip(SSH, guardv["ssh"]):
                            sc.wait_ge(c.h, v)
                    if need_kv:
                        sc.wait_ge(S["kv"].h, kv_dma_done)
                    sc.activation(out=dst, in_=pa, func=AF.Silu
                                  ).then_inc(evA.h, 1)
                em("scalar", fs)
                pa_free(slot, evA)
                if s >= 4 and tc in (3, 11):
                    drain_tp(1, lag=1)
                    drain_cmp(1)
                if s == 3 and tc == NCH - 1:
                    while g_cmp[0] < min(NUNITS, GATHER_LIMIT):
                        drain_tp(1, lag=1)
                        drain_cmp(1)
                    drain_tp(NUNITS + 2, lag=0)
                    guardv["gac"] = gac.n
                    guardv["tpe"] = tpe.n
                    guardv["ssh"] = [c.n for c in SSH]
                return None

            proj_sweeps("wu", 64, post_u, ready=lf_tev,
                            swp=16, wb_base=SD_U, wb_sz=2048,
                            s_order=range(15, -1, -1))
            MM_AFTER_U = mm.n
            if stage_n == 3:
                dbg_dump(SC_U, [(evA.h, evA.n)])

            WO_WB = {0: WB_WO, 1: WB_WO + 4096, 2: LF_TBF, 3: LF_XQ,
                     4: SD_U, 5: WB_WO, 6: WB_WO + 4096, 7: LF_TBF}

            def f_ccr(g):
                g.wait_ge(S["kv"].h, kv_dma_done)
                g.collective_compute(
                    "AllReduce", ALU.add,
                    replica_groups=[[0, 1], [2, 3], [4, 5], [6, 7]],
                    ins=[kv_in.ap().opt()], outs=[kv_out.ap().opt()],
                ).then_inc(S["cc"].h, 1)
            em("gpsimd", f_ccr)
            S["cc"].inc(1)

            def f_gb(g, mg=MM_AFTER_U):
                g.wait_ge(mm.h, mg)
                g.wait_ge(S["cc"].h, 1)
                g.dma_start(out=ab(LF_KVB, 2048), in_=kv_out.ap()[:, :]
                            ).then_inc(S["sgb"].h, 16)
                g.dma_start(out=ab(LF_GB, 2048),
                            in_=vecs["lng"].broadcast_to([128, D])
                            ).then_inc(S["sgb"].h, 16)
                g.dma_start(out=ab(LF_GB + 2048, 2048),
                            in_=vecs["lnb"].broadcast_to([128, D])
                            ).then_inc(S["sgb"].h, 16)
            em("gpsimd", f_gb)
            S["sgb"].inc(48)
            CST2 = S["sgb"].n

            for s_ in range(4):
                emit_sweep_load("wo", s_, SWN, WB_WO, 4096,
                                extra_w_wait=(mm.h, MM_AFTER_U), wb_map=WO_WB)
            emit_sweep_load("wo", 4, SWN, WB_WO, 4096, wb_map=WO_WB,
                            rel_override=(mm.h, MM_AFTER_U))

            def f_cca(g):
                for c, v in zip(SSH, guardv["ssh"]):
                    g.wait_ge(c.h, v)
                g.wait_ge(gac.h, guardv["gac"])
                g.collective_compute(
                    "AllGather", ALU.bypass,
                    replica_groups=[[0, 1, 2, 3, 4, 5, 6, 7]],
                    ins=[cc_in.ap().opt()], outs=[cc_out.ap().opt()],
                ).then_inc(S["cc"].h, 1)
            em("gpsimd", f_cca)
            S["cc"].inc(1)
            CC_ALL = S["cc"].n

            fzc = S["fz"]

            def f_pdall(te):
                te.wait_ge(pd.h, PD_ALL)
            em("tensor", f_pdall)
            pend_tu = []
            uln_tev = {}
            ffz_tc = {}
            for tc in range(NCH):
                while len(pend_tu) > 2:
                    pend_tu.pop(0)()
                pf = LF_PFM + (tc % 2) * 2048
                for k0 in range(0, KCN, 4):
                    transpose_block(
                        [ab(PBF + tc * 2048 + kc * 128, 128)
                         for kc in range(k0, k0 + 4)],
                        batch_dst=ab(pf + k0 * 128, 512))
                tev_need = tev.n
                gi_e = S["evE"].n

                def fe(te, tc=tc, pf=pf, tev_need=tev_need, gi_e=gi_e):
                    te.wait_ge(tev.h, tev_need)
                    te.wait_ge(S["sgb"].h, CST2)
                    if tc >= 1:
                        te.wait_ge(S["evE"].h, gi_e)
                    ins = None
                    for h in range(HEADS):
                        ins = te.matmul(
                            PBp[:, h * 128:(h + 1) * 128],
                            ab(pf + h * 128, 128),
                            AB[:, LF_KVB + kvcol(h):LF_KVB + kvcol(h) + 128],
                            start=True, stop=True)
                    ins.then_inc(mm.h, 1)
                em("tensor", fe)
                mm.inc()
                mmv = mm.n
                lV, st = ln_chunk(PBp[:], None, LF_TRASH, x_wait=(mm.h, mmv),
                                  emit_norm=False)
                atb = ab(AT_BF + (tc % 2) * 2048, 2048)
                ffz_prev = ffz_tc[tc - 2] if tc >= 2 else 0

                def fev(sc, atb=atb, fp=ffz_prev):
                    if fp:
                        sc.wait_ge(fzc.h, fp)
                    sc.activation(out=atb, in_=PBp[:], func=AF.Copy
                                  ).then_inc(S["evE"].h, 1)
                em("scalar", fev)
                S["evE"].inc()
                ub = ab(PBF + tc * 2048, 2048)

                def ff(v, tc=tc, st=st, lV=lV, ub=ub, atb=atb,
                       eE=S["evE"].n):
                    v.wait_ge(S["ln"].h, lV)
                    v.wait_ge(S["evE"].h, eE)
                    v.tensor_scalar(out=ub, in0=atb, scalar1=st[:, 0:1],
                                    scalar2=st[:, 5:6], op0=ALU.subtract,
                                    op1=ALU.mult).then_inc(evb.h, 1)
                    v.tensor_tensor(out=ub, in0=ub, in1=ab(LF_GB, 2048),
                                    op=ALU.mult)
                    v.tensor_tensor(out=ub, in0=ub, in1=ab(LF_GB + 2048, 2048),
                                    op=ALU.add)
                    AB3 = AB.ap().rearrange("p (s x) -> p s x", x=2048)
                    ub3 = AB.ap().rearrange("p (a x) -> p a x", x=128)
                    ubo = PBF + tc * 2048
                    v.tensor_tensor(
                        out=ub3[:, ubo // 128:ubo // 128 + 16, :],
                        in0=ub3[:, ubo // 128:ubo // 128 + 16, :],
                        in1=AB3[:, SC_U // 2048:SC_U // 2048 + 16,
                                tc * 128:(tc + 1) * 128],
                        op=ALU.mult)
                    v.tensor_copy(out=stats[:, 56:60], in_=ub[0:128, 0:4]
                                  ).then_inc(fzc.h, 1)
                em("vector", ff)
                evb.inc()
                fzc.inc()
                fzv = fzc.n
                ffz_tc[tc] = fzv

                def fnm(v):
                    v.engine_nop().then_inc(nmc.h, 1)
                em("vector", fnm)
                nmc.inc()

                def mk_tu(tc=tc, fzv=fzv):
                    def go():
                        for k0 in range(0, KCN, 4):
                            transpose_block(
                                [ab(PBF + tc * 2048 + kc * 128, 128)
                                 for kc in range(k0, k0 + 4)],
                                waits=[(fzc.h, fzv)] if k0 == 0 else (),
                                batch_dst=ab(PBF + tc * 2048 + k0 * 128, 512))
                        uln_tev[tc] = tev.n
                    return go
                pend_tu.append(mk_tu())
            for fcb in pend_tu:
                fcb()
            pend_tu = []
            uln_all = tev.n
            if stage_n == 4:
                dbg_dump(PBF, [(tev.h, uln_all)])

            M0 = mm.n
            for s_, relv in ((5, 16), (6, 32), (7, 48)):
                emit_sweep_load("wo", s_, SWN, WB_WO, 4096, wb_map=WO_WB,
                                rel_override=(mm.h, M0 + relv))

            def post_o(tc, s, pa, mmv, slot, swn):
                def fv(v, pa=pa, mmv=mmv, tc=tc, s=s):
                    v.wait_ge(mm.h, mmv)
                    AB3 = AB.ap().rearrange("p (k x) -> p k x", x=2048)
                    pa3 = pa.rearrange("p (i x) -> p i x", x=128)
                    v.tensor_copy(
                        out=AB3[:, SC // 2048 + 2 * s:SC // 2048 + 2 * s + 2,
                                tc * 128:(tc + 1) * 128],
                        in_=pa3).then_inc(evV.h, 1)
                em("vector", fv)
                pa_free(slot, evV)
                return None

            def f_wog(te):
                te.wait_ge(S["sbo"].h, CST_BO)
            em("tensor", f_wog)
            proj_sweeps("wo", 0, post_o, lhsT_off=PBF,
                        ready=[uln_tev[t] for t in range(NCH)],
                        wb_base=WB_WO, extra_w_wait=(mm.h, MM_AFTER_U),
                        lhsT_tcmajor=True, wb_map=WO_WB)
            EV_WO = evV.n
            MM_AFTER_WO = mm.n

            ABF3 = abf(0, ABLEN).rearrange("p (k x) -> p k x", x=128)
            X1K = (LF_X1C // 2) // 128
            for tc in range(NCH):
                nm_prev = nmc.n

                def fl(sy, tc=tc, nm_prev=nm_prev,
                       stv=SX1[(tc - 1) % 2].n):
                    if tc == 0:
                        sy.wait_ge(fzc.h, ffz_tc[NCH - 1])
                    if tc >= 1:
                        sy.wait_ge(nmc.h, nm_prev)
                        sy.wait_ge(SX1[(tc - 1) % 2].h, stv)
                    sy.dma_start(out=abf(LF_X1C, 4096),
                                 in_=x_in[tc * 128:(tc + 1) * 128, :]
                                 ).then_inc(S["sx2"].h, 16)
                em("sync", fl)
                S["sx2"].inc(16)
                xwv = S["sx2"].n

                def fadd(v, tc=tc, xwv=xwv):
                    v.wait_ge(S["sx2"].h, xwv)
                    v.wait_ge(evV.h, EV_WO)
                    AB3 = AB.ap().rearrange("p (k x) -> p k x", x=2048)
                    v.tensor_tensor(
                        out=ABF3[:, X1K:X1K + 16, :],
                        in0=ABF3[:, X1K:X1K + 16, :],
                        in1=AB3[:, SC // 2048:SC // 2048 + 16,
                                tc * 128:(tc + 1) * 128],
                        op=ALU.add).then_inc(evV.h, 1)
                em("vector", fadd)
                evV.inc()
                adv = evV.n

                def fsx(sy, tc=tc, adv=adv):
                    sy.wait_ge(evV.h, adv)
                    sy.dma_start(out=x1d[tc * 128:(tc + 1) * 128, :],
                                 in_=abf(LF_X1C, 4096)
                                 ).then_inc(SX1[tc % 2].h, 16)
                    if STAGE == "x1":
                        sy.dma_start(out=y[tc * 128:(tc + 1) * 128, :],
                                     in_=abf(LF_X1C, 4096)
                                     ).then_inc(SX1[tc % 2].h, 16)
                em("sync", fsx)
                SX1[tc % 2].inc(32 if STAGE == "x1" else 16)

                nmv = ln_chunk(abf(LF_X1C, 4096), ab(LF_LN2S, 2048), LF_TRASH2,
                               x_wait=(evV.h, adv),
                               norm_wait=(tpe.h, prev_tr[0]) if tc else None)
                ARB = AB.ap().rearrange("p (a r) -> p a r", r=2048)
                for k0 in range(0, KCN, 4):
                    transpose_block(
                        [ab(LF_LN2S + kc * 128, 128)
                         for kc in range(k0, k0 + 4)],
                        waits=[(nmc.h, nmv)] if k0 == 0 else (),
                        batch_dst=ARB[:, (SC_LN2F // 2048) + k0:
                                      (SC_LN2F // 2048) + k0 + 4,
                                      tc * 128:(tc + 1) * 128])
                prev_tr[0] = tpe.n
            LN2_ALL = tev.n
            if STAGE == "ln2":
                dbg_dump(SC_LN2F, [(tev.h, LN2_ALL)])
            if STAGE == "ffw":
                def fdw(g):
                    g.wait_ge(S["cc"].h, CC_ALL)
                    yf = y.rearrange("a d -> (a d)")
                    g.dma_start(out=yf[0:SHARD].rearrange("(a b) -> a b", b=2048),
                                in_=cc_out.ap()[0:1, :].rearrange(
                                    "a (r b) -> (a r) b", b=2048)
                                ).then_inc(S["yod"].h, 16)
                    g.dma_start(out=yf[SHARD:2 * SHARD].rearrange(
                                    "(a b) -> a b", b=2048),
                                in_=(cc_in if DUMP_CCIN else cc_out
                                     ).ap()[2:3, :].rearrange(
                                    "a (r b) -> (a r) b", b=2048)
                                ).then_inc(S["yod"].h, 16)
                    g.wait_ge(S["yod"].h, 32)
                em("gpsimd", fdw)
                S["yod"].inc(32)
                raise StopEmit()

            X1_ALL = [c.n for c in SX1]
            if STAGE == "x1":
                def f_fin(sy, xa=tuple(X1_ALL)):
                    for c, vv in zip(SX1, xa):
                        sy.wait_ge(c.h, vv)
                em("sync", f_fin)
            else:
                down_mm_prev_block = 0
                down_last = [0, 0, 0, 0]
                for blk_i in range(FBLK):
                    t0 = blk_i * FBT
                    evv_hlast = None
                    for ic in range(ICN):
                        gb = FF_GB + (ic % 2) * 4096
                        rel = MM_AFTER_WO if (blk_i == 0 and ic < 2) else None
                        mg = mm.n

                        ffg = FFG[ic % 2]
                        cc6 = cc_out.ap().rearrange(
                            "(r t) (s i p e) -> t s i p r e",
                            t=3, s=2, i=ICN, p=128)

                        def mk_fwl(mlist):
                            def fwl(sy, ic=ic, gb=gb, blk_i=blk_i, mg=mg,
                                    rel=rel, ffg=ffg, cc6=cc6, nmb=lnslot[0],
                                    mlist=mlist):
                                sy.wait_ge(S["cc"].h, CC_ALL)
                                if FFN_BARRIER and ic < 2:
                                    sy.wait_ge(nmc.h, nmb)
                                    sy.wait_ge(tev.h, LN2_ALL)
                                if FFN_SERIAL:
                                    sy.wait_ge(mm.h, mg)
                                elif rel is not None:
                                    sy.wait_ge(mm.h, rel)
                                elif blk_i > 0 or ic >= 2:
                                    sy.wait_ge(mm.h, mg - 4)
                                for m in mlist:
                                    V = AB[:, gb + m * 2048:
                                           gb + (m + 1) * 2048].rearrange(
                                        "p (k e) -> p k e", e=128)
                                    for s2 in range(2):
                                        sy.dma_start(out=V[:, s2::2, :],
                                                     in_=cc6[m, s2, ic]
                                                     ).then_inc(ffg.h, 16)
                            return fwl
                        em("sync", mk_fwl((0,)))
                        em("gpsimd", mk_fwl((1,)))
                        ffg.inc(16 * 4)
                        fwv = ffg.n
                        ginfo = [pa_alloc() for _ in range(4)]

                        def fg(te, ic=ic, gb=gb, t0=t0, fwv=fwv, ginfo=ginfo,
                               ln2_all=LN2_ALL, ffg=ffg):
                            te.wait_ge(ffg.h, fwv)
                            if ic == 0:
                                te.wait_ge(tev.h, ln2_all)
                            ins = None
                            for j in range(4):
                                mi, th = j // 2, j % 2
                                gi, slot, guard = ginfo[j]
                                pa = PA[:, slot * 512:(slot + 1) * 512]
                                if guard is not None:
                                    te.wait_ge(guard[0], guard[1])
                                for kc in range(KCN):
                                    ins = te.matmul(
                                        pa,
                                        AB[:, gb + mi * 2048 + kc * 128:
                                           gb + mi * 2048 + (kc + 1) * 128],
                                        ab(SC_LN2F + kc * 2048 + t0 + th * 512,
                                           512),
                                        start=(kc == 0), stop=(kc == KCN - 1))
                                ins.then_inc(mm.h, 1)
                        em("tensor", fg)
                        for j in range(4):
                            mm.inc()
                        mm_g = mm.n
                        for th in range(2):
                            sg = SE_SG + th * 512
                            gslot = ginfo[th][1]
                            uslot = ginfo[2 + th][1]
                            mmv_g = mm_g - 3 + th
                            mmv_u = mm_g - 1 + th

                            def fsg(sc, sg=sg, gslot=gslot, mmv=mmv_g):
                                sc.wait_ge(mm.h, mmv)
                                sc.activation(
                                    out=ab(sg, 512),
                                    in_=PA[:, gslot * 512:(gslot + 1) * 512],
                                    func=AF.Silu).then_inc(evA.h, 1)
                            em("scalar", fsg)
                            pa_free(gslot, evA)
                            ev = evA.n
                            dguard = down_mm_prev_block

                            def fh(v, sg=sg, uslot=uslot, ic=ic, th=th, ev=ev,
                                   mmv=mmv_u, dguard=dguard):
                                v.wait_ge(mm.h, mmv)
                                v.wait_ge(evA.h, ev)
                                if dguard:
                                    v.wait_ge(mm.h, dguard)
                                v.tensor_tensor(
                                    out=ab(FF_H + ic * 1024 + th * 512, 512),
                                    in0=PA[:, uslot * 512:(uslot + 1) * 512],
                                    in1=ab(sg, 512), op=ALU.mult
                                    ).then_inc(evV.h, 1)
                            em("vector", fh)
                            pa_free(uslot, evV)
                            evv_hlast = evV.n
                    for pr in range(4):
                        for fh_i in range(2):
                            gi_d = evb.n
                            for ic in range(ICN):
                                db = FF_DW + (ic % 4) * 1024
                                slot_guard = down_last[ic % 4]
                                g0r = ic * 128
                                r0 = g0r // IDXD
                                off = g0r - r0 * IDXD
                                take = min(128, IDXD - off)
                                pieces = [(r0, off, take)]
                                if take < 128:
                                    pieces.append((r0 + 1, 0, 128 - take))

                                ffd = FFD[ic % 4]

                                def fdl(sy, ic=ic, db=db, fh_i=fh_i,
                                        pieces=tuple(pieces), ffd=ffd,
                                        slot_guard=slot_guard):
                                    sy.wait_ge(S["cc"].h, CC_ALL)
                                    if slot_guard == 0:
                                        sy.wait_ge(mm.h, MM_AFTER_WO)
                                    else:
                                        sy.wait_ge(mm.h, slot_guard)
                                    d0 = 0
                                    for (r, off2, tk) in pieces:
                                        src = cc_out.ap()[3 * r + 2:3 * r + 3,
                                                          :].rearrange(
                                            "a (i e) -> (a i) e", i=IDXD)[
                                            off2:off2 + tk,
                                            fh_i * 1024:(fh_i + 1) * 1024]
                                        sy.dma_start(out=AB[d0:d0 + tk,
                                                            db:db + 1024],
                                                     in_=src).then_inc(ffd.h, 16)
                                        d0 += tk
                                em("gpsimd", fdl)
                                ffd.inc(16 * len(pieces))
                                fwv = ffd.n

                                def fdm(te, ic=ic, db=db, pr=pr, fwv=fwv,
                                        gi_d=gi_d, hv=evv_hlast, ffd=ffd):
                                    te.wait_ge(ffd.h, fwv)
                                    if ic == 0:
                                        te.wait_ge(evV.h, hv)
                                        te.wait_ge(evb.h, gi_d)
                                    ins = None
                                    for t2 in range(2):
                                        tc8 = pr * 2 + t2
                                        for nn2 in range(2):
                                            ins = te.matmul(
                                                PBp[:, (t2 * 2 + nn2) * 512:
                                                    (t2 * 2 + nn2 + 1) * 512],
                                                ab(FF_H + ic * 1024 + tc8 * 128,
                                                   128),
                                                AB[:, db + nn2 * 512:
                                                   db + (nn2 + 1) * 512],
                                                start=(ic == 0),
                                                stop=(ic == ICN - 1))
                                    ins.then_inc(mm.h, 1)
                                em("tensor", fdm)
                                mm.inc()
                                down_last[ic % 4] = mm.n
                            mmv = mm.n
                            for t2 in range(2):
                                tc8g = t0 // 128 + pr * 2 + t2
                                x1b = FF_X1 + (tc8g % 2) * 4096
                                x1l = X1L[tc8g % 2]
                                if fh_i == 0:
                                    evg = evb.n

                                    def fx1(sy, tc8g=tc8g, x1b=x1b, evg=evg,
                                            x1l=x1l):
                                        for c, vv in zip(SX1, X1_ALL):
                                            sy.wait_ge(c.h, vv)
                                        if evg > 0:
                                            sy.wait_ge(evb.h, evg)
                                        sy.dma_start(
                                            out=abf(x1b, 4096),
                                            in_=x1d[tc8g * 128:(tc8g + 1) * 128,
                                                    :]).then_inc(x1l.h, 16)
                                    em("sync", fx1)
                                    x1l.inc(16)
                                xrv = x1l.n
                                x2b = FF_X2 + (t2 % 2) * 2048
                                syo = SYO[t2 % 2]
                                yg = syo.n

                                def fev(v, t2=t2, x1b=x1b, x2b=x2b, mmv=mmv,
                                        xrv=xrv, fh_i=fh_i, yg=yg, x1l=x1l,
                                        syo=syo):
                                    v.wait_ge(mm.h, mmv)
                                    v.wait_ge(x1l.h, xrv)
                                    if yg >= 16:
                                        v.wait_ge(syo.h, yg)
                                    v.tensor_tensor(
                                        out=abf(x2b, 2048),
                                        in0=PBp[:, t2 * 1024:(t2 + 1) * 1024],
                                        in1=abf(x1b + fh_i * 2048, 2048),
                                        op=ALU.add).then_inc(evb.h, 1)
                                em("vector", fev)
                                evb.inc()
                                ev2 = evb.n

                                def fyo(sy, tc8g=tc8g, x2b=x2b, fh_i=fh_i,
                                        ev2=ev2, syo=syo):
                                    sy.wait_ge(evb.h, ev2)
                                    sy.dma_start(
                                        out=y[tc8g * 128:(tc8g + 1) * 128,
                                              fh_i * 1024:(fh_i + 1) * 1024],
                                        in_=abf(x2b, 2048)).then_inc(syo.h, 16)
                                em("sync", fyo)
                                syo.inc(16)
                    down_mm_prev_block = mm.n
                yo_tot = [c.n for c in SYO]

                def f_fin(sy, yt=tuple(yo_tot)):
                    for c, vv in zip(SYO, yt):
                        sy.wait_ge(c.h, vv)
                em("sync", f_fin)


        except StopEmit:
            pass

        blk = es.enter_context(nc.Block())

        @blk.sync
        def _(h):
            for f in Q["sync"]:
                f(h)

        @blk.tensor
        def _(h):
            for f in Q["tensor"]:
                f(h)

        @blk.vector
        def _(h):
            for f in Q["vector"]:
                f(h)

        @blk.scalar
        def _(h):
            for f in Q["scalar"]:
                f(h)

        @blk.gpsimd
        def _(h):
            for f in Q["gpsimd"]:
                f(h)

    return nc


def kernel(L0_attn_x, L0_attn_Wqk, L0_attn_bqk, L0_attn_Wv, L0_attn_bv,
           L0_attn_Wu, L0_attn_bu, L0_attn_Wo, L0_attn_bo, L0_attn_ln_g,
           L0_attn_ln_b, L0_ffn_lut, L0_ffn_gate_idx, L0_ffn_up_idx,
           L0_ffn_down_idx):
    x = np.ascontiguousarray(np.asarray(L0_attn_x, dtype=np.float32)
                             ).reshape(NCORES * TOK, D)
    base = dict(
        wqk=np.ascontiguousarray(np.asarray(L0_attn_Wqk, dtype=np.float32)),
        wv=np.ascontiguousarray(np.asarray(L0_attn_Wv, dtype=np.float32)),
        wu=np.ascontiguousarray(np.asarray(L0_attn_Wu, dtype=np.float32)),
        wo=np.ascontiguousarray(np.asarray(L0_attn_Wo, dtype=np.float32)),
        bqk=np.asarray(L0_attn_bqk, dtype=np.float32).reshape(1, D),
        bv=np.asarray(L0_attn_bv, dtype=np.float32).reshape(1, D),
        bu=np.asarray(L0_attn_bu, dtype=np.float32).reshape(1, D),
        bo=np.asarray(L0_attn_bo, dtype=np.float32).reshape(1, D),
        lng=np.asarray(L0_attn_ln_g, dtype=np.float32).reshape(1, D),
        lnb=np.asarray(L0_attn_ln_b, dtype=np.float32).reshape(1, D),
        lut=np.asarray(L0_ffn_lut, dtype=np.float32).reshape(1, 16),
    )
    g_idx = np.asarray(L0_ffn_gate_idx, dtype=np.int32)
    u_idx = np.asarray(L0_ffn_up_idx, dtype=np.int32)
    d_idx = np.asarray(L0_ffn_down_idx, dtype=np.int32)
    in_maps = []
    for c in range(NCORES):
        m = dict(base)
        m["x"] = np.ascontiguousarray(x[c * TOK:(c + 1) * TOK])
        m["gidx"] = np.ascontiguousarray(g_idx[:, c * IDXC:(c + 1) * IDXC])
        m["uidx"] = np.ascontiguousarray(u_idx[:, c * IDXC:(c + 1) * IDXC])
        m["didx"] = np.ascontiguousarray(d_idx[:, c * IDXD:(c + 1) * IDXD])
        in_maps.append(m)
    nc = build()
    res = run_bass_kernel_spmd(nc, in_maps, core_ids=list(range(NCORES)))
    out = np.concatenate([r["y"] for r in res.results], axis=0)
    return out.reshape(4, 4096, D)
